# version 09
# speedup vs baseline: 1.2559x; 1.0080x over previous
import numpy as np
import ml_dtypes
from contextlib import ExitStack

import concourse.bass as bass
import concourse.mybir as mybir
from concourse.bass_utils import run_bass_kernel_spmd

F32 = mybir.dt.float32
BF16 = mybir.dt.bfloat16
I32 = mybir.dt.int32
AF = mybir.ActivationFunctionType
ALU = mybir.AluOpType
AX = mybir.AxisListType
bfl = ml_dtypes.bfloat16
SYNC_LOG = None
F2_PREFETCH = True

D = 1024
H = 4
DK = 256
DV = 512
VD = 2048
CP = 256
NE = 16
EPS = 1e-5
ALPHA = 1.702
LIMIT = 7.0
GAMMA = [1.0 - 2.0 ** (-5 - h) for h in range(H)]
GC = [float(np.float64(g) ** CP) for g in GAMMA]
MAGICF = float(0x5F3759DF)
NEG = -1e30


def build_program(nch=32, phases=("F1a", "F1b", "P", "F2", "F3"), dbg_yacc=False, dbg_dump=False):
    T = nch * 128
    nsb = nch // 4
    chq = min(16, nch)
    nq = nch // chq
    nblk = chq // 4

    nc = bass.Bass("TRN2", target_bir_lowering=False)

    def din(name, shape, dt):
        return nc.dram_tensor(name, list(shape), dt, kind="ExternalInput").ap()

    x_own = din("x_own", (T, D), F32)
    x_pre = din("x_pre", (T, D), F32)
    WqTd = din("WqT", (D, 1024), F32)
    WkTd = din("WkT", (D, 1024), F32)
    WvTd = din("WvT", (D, VD), F32)
    WgTd = din("WgT", (D, VD), F32)
    WoTd = din("WoTd", (VD, D), F32)
    w1p = din("w1p", (NE, D, 2048), F32)
    w2d = din("w2d", (NE, 1024, D), F32)
    b1p = din("b1p", (NE, 2048), F32)
    b2d = din("b2d", (NE, D), F32)
    wgp = din("wgp", (D, 32), F32)
    bg32 = din("bg32", (1, 32), F32)
    gnw16 = din("gnw16", (16, 128), F32)
    ns_r = din("ns_r", (1, D), F32)
    ones1 = din("ones1", (1, 128), F32)
    ide_b = din("ide_b", (128, 128), BF16)
    ide_f = din("ide_f", (128, 128), F32)
    E16t = din("E16t", (16, 2048), F32)
    cosq = din("cosq", (128, T), BF16)
    sinq = din("sinq", (128, T), BF16)
    cosf = din("cosf", (128, T), BF16)
    sinf = din("sinf", (128, T), BF16)
    costm = din("costm", (T, 128), BF16)
    sintm = din("sintm", (T, 128), BF16)
    maskTi = din("maskTi", (2, 128, H, CP), BF16)
    kdeci = din("kdeci", (2, 128, H), F32)
    qgami = din("qgami", (128, H, CP), BF16)

    out = nc.dram_tensor("out", [T, D], F32, kind="ExternalOutput").ap()
    if dbg_dump:
        d_v = nc.dram_tensor("d_v", [nch, 128, VD], BF16, kind="ExternalOutput").ap()
        d_sig = nc.dram_tensor("d_sig", [nch, 128, VD], BF16, kind="ExternalOutput").ap()
        d_qT = nc.dram_tensor("d_qT", [nch // 4, 8, 128, 512], BF16, kind="ExternalOutput").ap()
        d_kT = nc.dram_tensor("d_kT", [nch // 4, 8, 128, 512], BF16, kind="ExternalOutput").ap()
        d_S = nc.dram_tensor("d_S", [128, H, 2, 512], F32, kind="ExternalOutput").ap()
        d_kp = nc.dram_tensor("d_kp", [128, 4, 1024], BF16, kind="ExternalOutput").ap()
        d_pk = nc.dram_tensor("d_pk", [128, 1024], F32, kind="ExternalOutput").ap()
        d_osc = nc.dram_tensor("d_osc", [nch, 128, VD], F32, kind="ExternalOutput").ap()
        d_x2 = nc.dram_tensor("d_x2", [nch // 4, 8, 128, 512], BF16, kind="ExternalOutput").ap()
        d_yT = nc.dram_tensor("d_yT", [128, chq // 4, 8, 512], F32, kind="ExternalOutput").ap()
        d_fs = nc.dram_tensor("d_fs", [128, chq // 4, 8, 512], BF16, kind="ExternalOutput").ap()
        d_xq = nc.dram_tensor("d_xq", [128, 8, chq * 128], BF16, kind="ExternalOutput").ap()
        d_ssq = nc.dram_tensor("d_ssq", [nch // 2, 128, 32], F32, kind="ExternalOutput").ap()
        d_ry = nc.dram_tensor("d_ry", [nch // 2, 128, 32], F32, kind="ExternalOutput").ap()
        d_pv = nc.dram_tensor("d_pv", [128, 4, VD], BF16, kind="ExternalOutput").ap()

    def dscr(name, shape, dt):
        return nc.dram_tensor(name, list(shape), dt, kind="Internal").ap()

    dr_xnT = dscr("dr_xnT", (nsb, 8, 128, 512), BF16)
    dr_qT = dscr("dr_qT", (nsb, 8, 128, 512), BF16)
    dr_kT = dscr("dr_kT", (nsb, 8, 128, 512), BF16)
    dr_v = dscr("dr_v", (nch, 128, VD), BF16)
    dr_sig = dscr("dr_sig", (nch, 128, VD), BF16)
    dr_x2T = dscr("dr_x2T", (nsb, 8, 128, 512), BF16)
    dr_yacc = dscr("dr_yacc", (nch, 128, D), F32)
    dr_S = dscr("dr_S", (128, H, 2, 512), F32)
    dr_Sb = dscr("dr_Sb", (128, H, 2, 512), BF16)

    AL = lambda n, s, d: nc.alloc_sbuf_tensor(n, list(s), d)

    c_ideb = AL("c_ideb", (128, 128), BF16)
    c_idef = AL("c_idef", (128, 128), F32)
    c_ones = AL("c_ones", (1, 128), F32)
    c_E16 = AL("c_E16", (16, 2048), F32)
    c_gnwT = AL("c_gnwT", (128, 16), F32)
    c_b2 = AL("c_b2", (16, D), F32)
    c_row = AL("c_row", (16, 128), F32)
    s_gateT = AL("s_gateT", (16, T), F32)
    r_a = AL("r_a", (128, 32), F32)
    r_sh = AL("r_sh", (128, 32), I32)
    r_shf = AL("r_shf", (128, 32), F32)
    r_zf = AL("r_zf", (128, 32), F32)
    r_zi = AL("r_zi", (128, 32), I32)
    r_y = AL("r_y", (128, 32), F32)
    r_t = AL("r_t", (128, 32), F32)
    r_y2 = AL("r_y2", (128, 32), F32)

    mark = nc.sbuf_base
    b_WkT = AL("b_WkT", (128, 8, 1024), BF16)
    b_WvT = AL("b_WvT", (128, 8, VD), BF16)
    mark2 = nc.sbuf_base

    def ln_group(tag):
        return dict(
            x4=AL(tag + "x4", (128, 4, D), F32),
            junk=AL(tag + "junk", (128, 4, D), F32),
            S1=AL(tag + "S1", (128, 32), F32),
            S2=AL(tag + "S2", (128, 32), F32),
            mean=AL(tag + "mean", (128, 32), F32),
            msq=AL(tag + "msq", (128, 32), F32),
            var=AL(tag + "var", (128, 32), F32),
            nmrs=AL(tag + "nmrs", (128, 32), F32),
            u=AL(tag + "u", (128, 32), F32),
            xn=AL(tag + "xn", (128, D), F32),
            xnb=AL(tag + "xnb", (128, D), BF16),
            xnT=AL(tag + "xnT", (128, 8, 512), BF16),
        )

    nc.sbuf_base = mark2
    L1 = ln_group("a_")
    c_wg = AL("c_wg", (128, 8, 32), F32)
    c_nsbc = AL("c_nsbc", (128, D), F32)
    c_bgbc = AL("c_bgbc", (128, 32), F32)
    s_x2 = AL("s_x2", (128, D), F32)
    s_x2s = AL("s_x2s", (128, D), F32)
    s_x2Tf = AL("s_x2Tf", (128, 2, 4, 128), F32)
    s_x2Tb = AL("s_x2Tb", (128, 8, 512), BF16)
    s_yst = AL("s_yst", (128, 2, D), F32)
    s_log = AL("s_log", (128, 4, 32), F32)
    s_mx = AL("s_mx", (128, 32), F32)
    s_hb = AL("s_hb", (128, 32), F32)
    s_th = AL("s_th", (128, 4, 32), F32)
    s_ea = AL("s_ea", (128, 4, 32), F32)
    s_eb = AL("s_eb", (128, 4, 32), F32)
    s_es = AL("s_es", (128, 4, 32), F32)
    s_sel = AL("s_sel", (128, 4, 32), F32)
    s_den = AL("s_den", (128, 32), F32)
    s_rden = AL("s_rden", (128, 32), F32)
    s_g16 = AL("s_g16", (128, 32), F32)
    assert nc.sbuf_base <= nc.sbuf_top

    nc.sbuf_base = mark2
    b_WqT = AL("b_WqT", (128, 8, 1024), BF16)
    b_WgT = AL("b_WgT", (128, 8, VD), BF16)
    b_xnT = AL("b_xnT", (128, 2, 8, 512), BF16)
    b_qT = AL("b_qT", (128, 8, 512), BF16)
    b_kT = AL("b_kT", (128, 8, 512), BF16)
    b_rt1 = AL("b_rt1", (128, 512), F32)
    b_rt2 = AL("b_rt2", (128, 512), F32)
    b_cq = AL("b_cq", (128, 2, 512), BF16)
    b_sq = AL("b_sq", (128, 2, 512), BF16)
    b_ck = AL("b_ck", (128, 2, 512), BF16)
    b_sk = AL("b_sk", (128, 2, 512), BF16)
    b_v = AL("b_v", (128, VD), BF16)
    b_sg = AL("b_sg", (128, VD), BF16)
    assert nc.sbuf_base <= nc.sbuf_top

    nc.sbuf_base = mark2
    LP = ln_group("p_")
    p_k = AL("p_k", (128, 1024), F32)
    p_kp = AL("p_kp", (128, 4, 1024), BF16)
    p_v = AL("p_v", (128, 4, VD), BF16)
    p_ct = AL("p_ct", (128, 128), BF16)
    p_st = AL("p_st", (128, 128), BF16)
    p_rt1 = AL("p_rt1", (128, 512), F32)
    p_rt2 = AL("p_rt2", (128, 512), F32)
    p_kdec = AL("p_kdec", (128, 2, H), F32)
    s_S = AL("s_S", (128, H, 2, 512), F32)
    s_Sb = AL("s_Sb", (128, H, 2, 512), BF16)
    assert nc.sbuf_base <= nc.sbuf_top

    nc.sbuf_base = mark
    f_w1s = AL("f_w1s", (128, 8, 2, 2, 128), BF16)
    f_w2s = AL("f_w2s", (128, 8, 2, 128), BF16)
    f_b1T = AL("f_b1T", (128, 32), F32)
    f_yst2 = AL("f_yst2", (128, 2, D), F32)
    f_x2q = AL("f_x2q", (128, 8, chq * 128), BF16)
    f_gbc = AL("f_gbc", (128, nblk, 512), F32)
    f_aS = AL("f_aS", (128, 2, 512), F32)
    f_bS = AL("f_bS", (128, 2, 512), F32)
    f_ac = AL("f_ac", (128, 2, 512), F32)
    f_p1 = AL("f_p1", (128, 2, 512), F32)
    f_s = AL("f_s", (128, nblk, 8, 512), BF16)
    f_yT = AL("f_yT", (128, nblk, 8, 512), F32)
    assert nc.sbuf_base <= nc.sbuf_top

    nc.sbuf_base = mark
    g_mask = AL("g_mask", (128, 2, H, CP), BF16)
    g_kdec = AL("g_kdec", (128, 2, H), F32)
    g_qgam = AL("g_qgam", (128, H, CP), BF16)
    g_S = AL("g_S", (128, H, 2, 512), F32)
    g_Sb = AL("g_Sb", (128, H, 2, 512), BF16)
    g_qT = AL("g_qT", (128, 8, 512), BF16)
    g_kT = AL("g_kT", (128, 8, 512), BF16)
    g_v = AL("g_v", (128, 2, VD), BF16)
    g_sig = AL("g_sig", (128, 2, VD), BF16)
    g_ktok = AL("g_ktok", (128, 2, 8, 128), BF16)
    g_qg = AL("g_qg", (128, 2, 256), BF16)
    g_am = AL("g_am", (128, H, 2, 256), BF16)
    g_osc = AL("g_osc", (128, 2, VD), F32)
    g_ssq = AL("g_ssq", (128, 32), F32)
    g_wst = AL("g_wst", (128, D), F32)
    g_onb = AL("g_onb", (128, VD), BF16)
    g_onT = AL("g_onT", (128, 16, 128), BF16)
    g_WoT = AL("g_WoT", (128, 16, D), BF16)
    g_yre = AL("g_yre", (128, 2, D), F32)
    g_fin = AL("g_fin", (128, 2, D), F32)
    g_junk = AL("g_junk", (128, 4, 512), F32)
    assert nc.sbuf_base <= nc.sbuf_top

    ctx = ExitStack()
    with ctx:
        pMM1 = ctx.enter_context(nc.psum_tensor("pMM1", [128, 512], F32))
        pMM2 = ctx.enter_context(nc.psum_tensor("pMM2", [128, 512], F32))
        pY1 = ctx.enter_context(nc.psum_tensor("pY1", [128, 512], F32))
        pY2 = ctx.enter_context(nc.psum_tensor("pY2", [128, 512], F32))
        pG1 = ctx.enter_context(nc.psum_tensor("pG1", [128, 512], F32))
        pXT = ctx.enter_context(nc.psum_tensor("pXT", [128, 2, 512], F32))
        pTb = ctx.enter_context(nc.psum_tensor("pTb", [128, 8, 128], BF16))

        sems = {s: ctx.enter_context(nc.semaphore(s))
                for s in ("LD", "ST", "GP", "PE", "VE", "AC",
                          "W0", "W1", "U0", "U1", "CR", "X2")}

        blk = ctx.enter_context(nc.Block())

        class Walk:
            def __init__(self, E, eng):
                self.E = E
                self.eng = eng
                self.c = {k: 0 for k in sems}
                self.mk = {}

            def _emit(self, engine, sem, n, builder):
                if sem is not None:
                    self.c[sem] += n
                if self.E == engine:
                    if SYNC_LOG is not None:
                        SYNC_LOG.append((self.E, "inc" if sem else "op", sem, n))
                    else:
                        ins = builder(self.eng)
                        if sem is not None:
                            ins.then_inc(sems[sem], n)

            def pe(self, b, inc=False):
                self._emit("pe", "PE" if inc else None, 1, b)

            def ve(self, b, inc=False):
                self._emit("ve", "VE" if inc else None, 1, b)

            def vc(self, b):
                self._emit("ve", "VE", 1, b)
                self.wait("ve", "VE")

            def ac(self, b, inc=False):
                self._emit("ac", "AC" if inc else None, 1, b)

            def gp(self, b, inc=False):
                self._emit("gp", "GP" if inc else None, 1, b)

            def dma(self, q, outap, inap, store=False, sem=None):
                if sem is None:
                    sem = "GP" if q == "gp" else ("ST" if store else "LD")
                self._emit(q, sem, 16, lambda e: e.dma_start(out=outap, in_=inap))

            def wait(self, engine, sem, val=None):
                if val is None:
                    val = self.c[sem]
                if self.E == engine:
                    if SYNC_LOG is not None:
                        SYNC_LOG.append((self.E, "wait", sem, val))
                    else:
                        self.eng.wait_ge(sems[sem], val)

            def mark(self, name):
                self.mk[name] = dict(self.c)

            def wmark(self, engine, sem, name):
                if name in self.mk:
                    self.wait(engine, sem, self.mk[name][sem])

            def barrier(self):
                for e in ("sy", "gp", "pe", "ve", "ac"):
                    for s in ("LD", "ST", "GP", "PE", "VE", "AC",
                              "W0", "W1", "U0", "U1", "CR", "X2"):
                        self.wait(e, s)

        def walk(w):
            V, P_, A, G = w.ve, w.pe, w.ac, w.gp

            w.dma("sy", c_ideb[...], ide_b[...])
            w.dma("sy", c_idef[...], ide_f[...])
            w.dma("sy", c_ones[...], ones1[...])
            w.dma("sy", c_E16[...], E16t[...])
            w.dma("sy", c_b2[...], b2d[...])
            w.dma("sy", c_row[...], gnw16[...])
            w.wait("pe", "LD")
            P_(lambda e: e.transpose(pXT[:, 0, 0:16], c_row[...], c_idef[0:16, 0:16]),
               inc=True)
            w.wait("ve", "PE")
            V(lambda e: e.tensor_copy(c_gnwT[...], pXT[:, 0, 0:16]), inc=True)
            w.barrier()

            def rsqrt_chain():
                Vc = w.vc
                Vc(lambda e: e.tensor_single_scalar(r_sh[...], r_a[...].bitcast(I32), 1,
                                                    ALU.arith_shift_right))
                w.wait("ac", "VE")
                A(lambda e: e.activation(r_shf[...], r_sh[...], AF.Identity), inc=True)
                w.wait("ve", "AC")
                Vc(lambda e: e.tensor_scalar(r_zf[...], r_shf[...], -1.0, MAGICF,
                                             ALU.mult, ALU.add))
                w.wait("ac", "VE")
                A(lambda e: e.activation(r_zi[...], r_zf[...], AF.Identity), inc=True)
                w.wait("ve", "AC")
                Vc(lambda e: e.tensor_copy(r_y[...], r_zi[...].bitcast(F32)))
                for _ in range(3):
                    Vc(lambda e: e.tensor_tensor(r_t[...], r_y[...], r_y[...], ALU.mult))
                    Vc(lambda e: e.tensor_tensor(r_t[...], r_t[...], r_a[...], ALU.mult))
                    Vc(lambda e: e.tensor_scalar(r_t[...], r_t[...], -0.5, 1.5,
                                                 ALU.mult, ALU.add))
                    Vc(lambda e: e.tensor_tensor(r_y[...], r_y[...], r_t[...], ALU.mult))

            def ln_stats(L, rms=True):
                w.wait("ac", "LD")
                w.wait("ve", "LD")
                w.wait("ac", "AC")
                for j in range(4):
                    A(lambda e, j=j: e.activation(L["junk"][:, j, :],
                                                  L["x4"][:, j, :],
                                                  AF.Square,
                                                  accum_out=L["S2"][:, j:j + 1]),
                      inc=(j == 3))
                for j in range(4):
                    V(lambda e, j=j: e.reduce_sum(L["S1"][:, j:j + 1], L["x4"][:, j, :],
                                                  axis=AX.X), inc=(j == 3))
                w.wait("ve", "VE")
                w.wait("ve", "AC")
                Vc = w.vc
                Vc(lambda e: e.tensor_scalar(L["mean"][...], L["S1"][...], 1.0 / D, None,
                                             ALU.mult))
                Vc(lambda e: e.tensor_scalar(L["msq"][...], L["S2"][...], 1.0 / D, None,
                                             ALU.mult))
                Vc(lambda e: e.tensor_tensor(L["var"][...], L["mean"][...],
                                             L["mean"][...], ALU.mult))
                Vc(lambda e: e.tensor_tensor(L["var"][...], L["msq"][...], L["var"][...],
                                             ALU.subtract))
                Vc(lambda e: e.tensor_scalar(r_a[...], L["var"][...], EPS, None, ALU.add))
                rsqrt_chain()
                Vc(lambda e: e.tensor_tensor(L["nmrs"][...], L["mean"][...], r_y[...],
                                             ALU.mult))
                Vc(lambda e: e.tensor_scalar(L["nmrs"][...], L["nmrs"][...], -1.0, None,
                                             ALU.mult))
                if rms:
                    Vc(lambda e: e.tensor_tensor(L["u"][...], r_y[...], r_y[...],
                                                 ALU.mult))
                    Vc(lambda e: e.tensor_scalar(L["u"][...], L["u"][...], -EPS,
                                                 1.0 + EPS, ALU.mult, ALU.add))
                    Vc(lambda e: e.tensor_scalar(r_y2[...], L["u"][...], -0.5, 1.5,
                                                 ALU.mult, ALU.add))
                    for _ in range(2):
                        Vc(lambda e: e.tensor_tensor(r_t[...], r_y2[...], r_y2[...],
                                                     ALU.mult))
                        Vc(lambda e: e.tensor_tensor(r_t[...], r_t[...], L["u"][...],
                                                     ALU.mult))
                        Vc(lambda e: e.tensor_scalar(r_t[...], r_t[...], -0.5, 1.5,
                                                     ALU.mult, ALU.add))
                        Vc(lambda e: e.tensor_tensor(r_y2[...], r_y2[...], r_t[...],
                                                     ALU.mult))
                V(lambda e: e.tensor_copy(r_t[...], r_y[...]), inc=True)

            def normalize_chunk(L, j):
                w.wait("ac", "VE")
                A(lambda e, j=j: e.activation(L["xn"][...], L["x4"][:, j, :],
                                              AF.Identity, scale=r_y[:, j:j + 1],
                                              bias=L["nmrs"][:, j:j + 1]), inc=True)
                w.wait("ve", "AC")
                V(lambda e, L=L: e.tensor_copy(L["xnb"][...], L["xn"][...]), inc=True)
                w.wait("pe", "VE")
                for fc in range(8):
                    P_(lambda e, fc=fc, L=L: e.transpose(
                        pTb[:, fc, :], L["xnb"][:, fc * 128:(fc + 1) * 128],
                        c_ideb[...]), inc=(fc == 7))
                w.wait("ve", "PE")
                V(lambda e, j=j, L=L: e.tensor_copy(
                    L["xnT"][:, :, j * 128:(j + 1) * 128], pTb[...]), inc=True)

            if "F1a" in phases:
                w.dma("sy", c_wg[...], wgp.rearrange("(fc p) e -> p fc e", p=128))
                w.dma("sy", c_nsbc[0:1, :], ns_r[...])
                w.dma("sy", c_bgbc[0:1, :], bg32[...])
                w.wait("pe", "LD")
                P_(lambda e: e.matmul(pMM1[...], c_ones[...], c_nsbc[0:1, 0:512],
                                      start=True, stop=True))
                P_(lambda e: e.matmul(pMM2[...], c_ones[...], c_nsbc[0:1, 512:1024],
                                      start=True, stop=True))
                P_(lambda e: e.matmul(pY1[:, 0:32], c_ones[...], c_bgbc[0:1, 0:32],
                                      start=True, stop=True), inc=True)
                w.wait("ve", "PE")
                V(lambda e: e.tensor_copy(c_nsbc[:, 0:512], pMM1[...]))
                V(lambda e: e.tensor_copy(c_nsbc[:, 512:1024], pMM2[...]))
                V(lambda e: e.tensor_copy(c_bgbc[...], pY1[:, 0:32]), inc=True)
                w.barrier()
                w.dma("gp", b_WkT[...], WkTd.rearrange("(fc p) n -> p fc n", p=128),
                      sem="W0")
                w.dma("gp", b_WvT[...], WvTd.rearrange("(fc p) n -> p fc n", p=128),
                      sem="W1")
                w.mark("kv_ld")
                for s in range(nsb):
                    c0 = 4 * s
                    w.wait("sy", "VE")
                    for j in range(4):
                        w.dma("sy", L1["x4"][:, j, :],
                              x_own[(c0 + j) * 128:(c0 + j + 1) * 128, :])
                    ln_stats(L1)
                    for j in range(4):
                        c = c0 + j
                        normalize_chunk(L1, j)
                        w.wait("ve", "ST")
                        V(lambda e, j=j: e.tensor_tensor(s_yst[:, j % 2, :],
                                                         L1["x4"][:, j, :],
                                                         L1["xn"][...], ALU.add),
                          inc=True)
                        w.wait("sy", "VE")
                        w.dma("sy", dr_yacc[c], s_yst[:, j % 2, :], store=True)
                        A(lambda e, j=j: e.activation(s_x2[...], L1["xn"][...],
                                                      AF.Identity,
                                                      scale=r_y2[:, j:j + 1]), inc=True)
                        w.wait("ve", "AC")
                        V(lambda e: e.tensor_tensor(s_x2s[...], s_x2[...], c_nsbc[...],
                                                    ALU.mult), inc=True)
                        w.wait("pe", "VE")
                        for fc in range(8):
                            P_(lambda e, fc=fc: e.transpose(
                                pXT[:, fc // 4, (fc % 4) * 128:(fc % 4 + 1) * 128],
                                s_x2s[:, fc * 128:(fc + 1) * 128], c_idef[...]),
                               inc=(fc == 7))
                        w.wait("ve", "PE")
                        V(lambda e: e.tensor_copy(
                            s_x2Tf[...], pXT[...].rearrange("p a (c b) -> p a c b",
                                                            b=128)), inc=True)
                        V(lambda e, j=j: e.tensor_copy(
                            s_x2Tb[:, :, j * 128:(j + 1) * 128]
                            .rearrange("p (a c) b -> p a c b", a=2), s_x2Tf[...]),
                          inc=True)
                        w.wait("pe", "VE")
                        for k in range(8):
                            P_(lambda e, k=k: e.matmul(pMM1[:, 0:32],
                                                       s_x2Tf[:, k // 4, k % 4, :],
                                                       c_wg[:, k, :], start=(k == 0),
                                                       stop=(k == 7)), inc=(k == 7))
                        w.wait("ve", "PE")
                        V(lambda e, j=j: e.tensor_tensor(s_log[:, j, :], pMM1[:, 0:32],
                                                         c_bgbc[...], ALU.add),
                          inc=True)
                        w.wait("pe", "VE")
                    w.wait("sy", "VE")
                    w.dma("sy", dr_xnT[s].rearrange("m p t -> p m t"), L1["xnT"][...],
                          store=True)
                    w.dma("sy", dr_x2T[s].rearrange("m p t -> p m t"), s_x2Tb[...],
                          store=True)
                    Vc = w.vc
                    for j in range(4):
                        V(lambda e, j=j: e.tensor_copy(s_sel[:, j, :], s_log[:, j, :]),
                          inc=True)
                    w.wait("ve", "VE")
                    for r in range(4):
                        for j in range(4):
                            V(lambda e, r=r, j=j: e.reduce_max(
                                s_mx[:, 4 * r + j:4 * r + j + 1], s_sel[:, j, :],
                                axis=AX.X), inc=True)
                        w.wait("ve", "VE")
                        if r < 3:
                            for j in range(4):
                                V(lambda e, r=r, j=j: e.tensor_scalar(
                                    s_es[:, j, :], s_sel[:, j, :],
                                    s_mx[:, 4 * r + j:4 * r + j + 1], None,
                                    ALU.is_equal), inc=True)
                            w.wait("ve", "VE")
                            for j in range(4):
                                V(lambda e, j=j: e.scalar_tensor_tensor(
                                    s_sel[:, j, :], s_es[:, j, :], NEG, s_sel[:, j, :],
                                    ALU.mult, ALU.add), inc=True)
                            w.wait("ve", "VE")
                    for j in range(4):
                        V(lambda e, j=j: e.tensor_scalar(s_sel[:, j, :], s_log[:, j, :],
                                                         s_mx[:, 12 + j:13 + j], None,
                                                         ALU.is_ge), inc=True)
                    Vc(lambda e: e.tensor_scalar(s_hb[...], s_mx[...], -0.5, None,
                                                 ALU.mult))
                    w.wait("ac", "VE")
                    for j in range(4):
                        A(lambda e, j=j: e.activation(s_th[:, j, :], s_log[:, j, :],
                                                      AF.Tanh, scale=0.5,
                                                      bias=s_hb[:, j:j + 1]),
                          inc=(j == 3))
                    w.wait("ve", "AC")
                    for j in range(4):
                        V(lambda e, j=j: e.tensor_scalar(s_ea[:, j, :], s_th[:, j, :],
                                                         1.0, None, ALU.add), inc=True)
                    for j in range(4):
                        V(lambda e, j=j: e.tensor_scalar(s_eb[:, j, :], s_th[:, j, :],
                                                         -1.0, 1.0, ALU.mult, ALU.add),
                          inc=True)
                    w.wait("ve", "VE")
                    for j in range(4):
                        V(lambda e, j=j: e.reciprocal(s_es[:, j, :], s_eb[:, j, :]),
                          inc=True)
                    w.wait("ve", "VE")
                    for j in range(4):
                        V(lambda e, j=j: e.tensor_tensor(s_ea[:, j, :], s_ea[:, j, :],
                                                         s_es[:, j, :], ALU.mult),
                          inc=True)
                    w.wait("ve", "VE")
                    for j in range(4):
                        V(lambda e, j=j: e.tensor_tensor(s_ea[:, j, :], s_ea[:, j, :],
                                                         s_sel[:, j, :], ALU.mult),
                          inc=True)
                    w.wait("ve", "VE")
                    for j in range(4):
                        V(lambda e, j=j: e.reduce_sum(s_den[:, j:j + 1], s_ea[:, j, :],
                                                      axis=AX.X), inc=True)
                    w.wait("ve", "VE")
                    Vc(lambda e: e.reciprocal(s_rden[...], s_den[...]))
                    for j in range(4):
                        w.wait("ac", "VE")
                        A(lambda e, j=j: e.activation(s_g16[...], s_ea[:, j, :],
                                                      AF.Identity,
                                                      scale=s_rden[:, j:j + 1]),
                          inc=True)
                        w.wait("pe", "AC")
                        P_(lambda e: e.transpose(pG1[0:16, 0:128], s_g16[:, 0:16],
                                                 c_idef[...]), inc=True)
                        w.wait("ve", "PE")
                        V(lambda e, j=j, c0=c0: e.tensor_copy(
                            s_gateT[:, (c0 + j) * 128:(c0 + j + 1) * 128],
                            pG1[0:16, 0:128]), inc=True)
                        w.wait("pe", "VE")
                    w.barrier()

            if "F1b" in phases:
                w.dma("gp", b_WqT[...], WqTd.rearrange("(fc p) n -> p fc n", p=128),
                      sem="U0")
                w.mark("wq_ld")
                w.dma("gp", b_WgT[...], WgTd.rearrange("(fc p) n -> p fc n", p=128),
                      sem="U1")
                w.mark("wg_ld")
                w.dma("sy", b_xnT[:, 0, :, :], dr_xnT[0].rearrange("m p t -> p m t"))
                w.dma("sy", b_cq[:, 0, :], cosq[:, 0:512])
                w.dma("sy", b_sq[:, 0, :], sinq[:, 0:512])
                w.dma("sy", b_ck[:, 0, :], cosf[:, 0:512])
                w.dma("sy", b_sk[:, 0, :], sinf[:, 0:512])
                w.mark("bld_0")
                for s in range(nsb):
                    c0 = 4 * s
                    sp = s % 2
                    xnT = b_xnT[:, sp, :, :]
                    if s + 1 < nsb:
                        t0 = (s + 1) * 512
                        if s >= 1:
                            w.wmark("sy", "PE", f"gB_{s - 1}_3")
                            w.wmark("sy", "VE", f"rope_{s - 1}_7")
                        w.dma("sy", b_xnT[:, 1 - sp, :, :],
                              dr_xnT[s + 1].rearrange("m p t -> p m t"))
                        w.dma("sy", b_cq[:, 1 - sp, :], cosq[:, t0:t0 + 512])
                        w.dma("sy", b_sq[:, 1 - sp, :], sinq[:, t0:t0 + 512])
                        w.dma("sy", b_ck[:, 1 - sp, :], cosf[:, t0:t0 + 512])
                        w.dma("sy", b_sk[:, 1 - sp, :], sinf[:, t0:t0 + 512])
                        w.mark(f"bld_{s + 1}")
                    for h in range(H):
                        for ki, (wT, dst) in enumerate([(b_WqT, b_qT),
                                                        (b_WkT, b_kT)]):
                            u = 2 * h + ki
                            ct = (b_cq if ki == 0 else b_ck)[:, sp, :]
                            st = (b_sq if ki == 0 else b_sk)[:, sp, :]
                            pe1 = pMM1 if u % 2 == 0 else pY1
                            pe2 = pMM2 if u % 2 == 0 else pY2
                            m1, m2 = 2 * h, 2 * h + 1
                            if u == 0:
                                w.wmark("pe", "LD", f"bld_{s}")
                                w.wmark("pe", "U0", "wq_ld")
                                w.wmark("pe", "W0", "kv_ld")
                                w.wmark("pe", "W1", "kv_ld")
                                if s == 0:
                                    w.wmark("pe", "AC", "gB_m1")
                                else:
                                    w.wmark("pe", "AC", f"gA_{s - 1}_3")
                            if u == 1 and s > 0:
                                w.wmark("pe", "AC", f"gB_{s - 1}_3e")
                            if u >= 2:
                                w.wmark("pe", "VE", f"rope_{s}_{u - 2}")
                            for k in range(8):
                                P_(lambda e, k=k, m1=m1, wT=wT, pe1=pe1, xnT=xnT:
                                   e.matmul(pe1[...],
                                            wT[:, k, m1 * 128:(m1 + 1) * 128],
                                            xnT[:, k, :], start=(k == 0),
                                            stop=(k == 7)))
                            for k in range(8):
                                P_(lambda e, k=k, m2=m2, wT=wT, pe2=pe2, xnT=xnT:
                                   e.matmul(pe2[...],
                                            wT[:, k, m2 * 128:(m2 + 1) * 128],
                                            xnT[:, k, :], start=(k == 0),
                                            stop=(k == 7)), inc=(k == 7))
                            w.mark(f"qk_{s}_{u}")
                            w.wmark("ve", "PE", f"qk_{s}_{u}")
                            if u <= 1:
                                w.wait("ve", "ST")
                            V(lambda e, ct=ct, pe1=pe1: e.tensor_tensor(
                                b_rt1[...], pe1[...], ct, ALU.mult))
                            V(lambda e, st=st, pe2=pe2: e.tensor_tensor(
                                b_rt2[...], pe2[...], st, ALU.mult))
                            V(lambda e, dst=dst, m1=m1: e.tensor_tensor(
                                dst[:, m1, :], b_rt1[...], b_rt2[...],
                                ALU.subtract))
                            V(lambda e, ct=ct, pe2=pe2: e.tensor_tensor(
                                b_rt1[...], pe2[...], ct, ALU.mult))
                            V(lambda e, st=st, pe1=pe1: e.tensor_tensor(
                                b_rt2[...], pe1[...], st, ALU.mult))
                            V(lambda e, dst=dst, m2=m2: e.tensor_tensor(
                                dst[:, m2, :], b_rt1[...], b_rt2[...], ALU.add),
                              inc=True)
                            w.mark(f"rope_{s}_{u}")
                    w.wmark("sy", "VE", f"rope_{s}_7")
                    w.dma("sy", dr_qT[s].rearrange("m p t -> p m t"), b_qT[...],
                          store=True)
                    w.dma("sy", dr_kT[s].rearrange("m p t -> p m t"), b_kT[...],
                          store=True)
                    for j in range(4):
                        c = c0 + j
                        if j == 0:
                            w.wmark("pe", "VE", f"rope_{s}_6")
                        else:
                            w.wmark("pe", "AC", f"gA_{s}_{j - 1}")
                        for bk in range(2):
                            pb = pMM1 if bk == 0 else pMM2
                            for k in range(8):
                                P_(lambda e, k=k, bk=bk, pb=pb, j=j, xnT=xnT:
                                   e.matmul(pb[...],
                                            xnT[:, k, j * 128:(j + 1) * 128],
                                            b_WvT[:, k, bk * 512:(bk + 1) * 512],
                                            start=(k == 0), stop=(k == 7)),
                                   inc=(k == 7 and bk == 1))
                        w.mark(f"vA_{s}_{j}")
                        if j == 0:
                            w.wmark("pe", "VE", f"rope_{s}_7")
                        else:
                            w.wmark("pe", "AC", f"gB_{s}_{j - 1}e")
                        for bk in range(2, 4):
                            pb = pY1 if bk == 2 else pY2
                            for k in range(8):
                                P_(lambda e, k=k, bk=bk, pb=pb, j=j, xnT=xnT:
                                   e.matmul(pb[...],
                                            xnT[:, k, j * 128:(j + 1) * 128],
                                            b_WvT[:, k, bk * 512:(bk + 1) * 512],
                                            start=(k == 0), stop=(k == 7)),
                                   inc=(k == 7 and bk == 3))
                        w.mark(f"vB_{s}_{j}")
                        w.wmark("ve", "PE", f"vA_{s}_{j}")
                        w.wait("ve", "ST")
                        for bk in range(2):
                            pb = pMM1 if bk == 0 else pMM2
                            V(lambda e, bk=bk, pb=pb: e.tensor_copy(
                                b_v[:, bk * 512:(bk + 1) * 512], pb[...]),
                              inc=(bk == 1))
                        w.mark(f"vAe_{s}_{j}")
                        w.wmark("ve", "PE", f"vB_{s}_{j}")
                        for bk in range(2, 4):
                            pb = pY1 if bk == 2 else pY2
                            V(lambda e, bk=bk, pb=pb: e.tensor_copy(
                                b_v[:, bk * 512:(bk + 1) * 512], pb[...]),
                              inc=(bk == 3))
                        w.mark(f"vBe_{s}_{j}")
                        w.wmark("sy", "VE", f"vBe_{s}_{j}")
                        w.dma("sy", dr_v[c], b_v[...], store=True)
                        w.wmark("pe", "VE", f"vAe_{s}_{j}")
                        if j == 0:
                            w.wmark("pe", "U1", "wg_ld")
                        for bk in range(2):
                            pb = pMM1 if bk == 0 else pMM2
                            for k in range(8):
                                P_(lambda e, k=k, bk=bk, pb=pb, j=j, xnT=xnT:
                                   e.matmul(pb[...],
                                            xnT[:, k, j * 128:(j + 1) * 128],
                                            b_WgT[:, k, bk * 512:(bk + 1) * 512],
                                            start=(k == 0), stop=(k == 7)),
                                   inc=(k == 7 and bk == 1))
                        w.mark(f"gA_{s}_{j}m")
                        w.wmark("pe", "VE", f"vBe_{s}_{j}")
                        for bk in range(2, 4):
                            pb = pY1 if bk == 2 else pY2
                            for k in range(8):
                                P_(lambda e, k=k, bk=bk, pb=pb, j=j, xnT=xnT:
                                   e.matmul(pb[...],
                                            xnT[:, k, j * 128:(j + 1) * 128],
                                            b_WgT[:, k, bk * 512:(bk + 1) * 512],
                                            start=(k == 0), stop=(k == 7)),
                                   inc=(k == 7 and bk == 3))
                        w.mark(f"gB_{s}_{j}")
                        w.wmark("ac", "PE", f"gA_{s}_{j}m")
                        w.wait("ac", "ST")
                        for bk in range(2):
                            pb = pMM1 if bk == 0 else pMM2
                            A(lambda e, bk=bk, pb=pb: e.activation(
                                b_sg[:, bk * 512:(bk + 1) * 512], pb[...],
                                AF.Silu), inc=(bk == 1))
                        w.mark(f"gA_{s}_{j}")
                        w.wmark("ac", "PE", f"gB_{s}_{j}")
                        for bk in range(2, 4):
                            pb = pY1 if bk == 2 else pY2
                            A(lambda e, bk=bk, pb=pb: e.activation(
                                b_sg[:, bk * 512:(bk + 1) * 512], pb[...],
                                AF.Silu), inc=(bk == 3))
                        w.mark(f"gB_{s}_{j}e")
                        w.wmark("sy", "AC", f"gB_{s}_{j}e")
                        w.dma("sy", dr_sig[c], b_sg[...], store=True)
                w.barrier()

            if "P" in phases:
                if "F1b" not in phases and "F1a" not in phases:
                    w.dma("gp", b_WkT[...],
                          WkTd.rearrange("(fc p) n -> p fc n", p=128), sem="W0")
                    w.dma("gp", b_WvT[...],
                          WvTd.rearrange("(fc p) n -> p fc n", p=128), sem="W1")
                w.dma("sy", p_kdec[...], kdeci.transpose([1, 0, 2]))
                V(lambda e: e.memset(s_S[...], 0.0), inc=True)
                w.barrier()
                for s in range(nsb):
                    c0 = 4 * s
                    w.wait("sy", "VE")
                    for j in range(4):
                        w.dma("sy", LP["x4"][:, j, :],
                              x_pre[(c0 + j) * 128:(c0 + j + 1) * 128, :])
                    ln_stats(LP, rms=False)
                    for j in range(4):
                        c = c0 + j
                        normalize_chunk(LP, j)
                        w.wait("sy", "VE")
                        w.dma("sy", p_ct[...], costm[c * 128:(c + 1) * 128, :])
                        w.dma("sy", p_st[...], sintm[c * 128:(c + 1) * 128, :])
                        w.wait("pe", "VE")
                        for nb, pb in enumerate([pMM1, pMM2]):
                            for k in range(8):
                                P_(lambda e, k=k, nb=nb, pb=pb, j=j: e.matmul(
                                    pb[...], LP["xnT"][:, k, j * 128:(j + 1) * 128],
                                    b_WkT[:, k, nb * 512:(nb + 1) * 512],
                                    start=(k == 0), stop=(k == 7)),
                                   inc=(k == 7 and nb == 1))
                        w.wait("ve", "PE")
                        w.wait("ve", "LD")
                        w.vc(lambda e: e.tensor_copy(p_k[:, 0:512], pMM1[...]))
                        w.vc(lambda e: e.tensor_copy(p_k[:, 512:1024], pMM2[...]))
                        w.wait("pe", "VE")
                        for h in range(H):
                            hb = h * 256
                            V(lambda e, hb=hb: e.tensor_tensor(
                                p_rt1[:, 0:128], p_k[:, hb:hb + 128], p_ct[...],
                                ALU.mult))
                            V(lambda e, hb=hb: e.tensor_tensor(
                                p_rt1[:, 128:256], p_k[:, hb + 128:hb + 256],
                                p_st[...], ALU.mult))
                            V(lambda e, hb=hb: e.tensor_tensor(
                                p_rt2[:, 0:128], p_k[:, hb + 128:hb + 256], p_ct[...],
                                ALU.mult))
                            V(lambda e, hb=hb: e.tensor_tensor(
                                p_rt2[:, 128:256], p_k[:, hb:hb + 128], p_st[...],
                                ALU.mult))
                            V(lambda e, hb=hb: e.tensor_tensor(
                                p_k[:, hb:hb + 128], p_rt1[:, 0:128],
                                p_rt1[:, 128:256], ALU.subtract))
                            V(lambda e, hb=hb: e.tensor_tensor(
                                p_k[:, hb + 128:hb + 256], p_rt2[:, 0:128],
                                p_rt2[:, 128:256], ALU.add), inc=(h == H - 1))
                        for h in range(H):
                            V(lambda e, h=h, j=j: e.tensor_scalar(
                                p_kp[:, j, h * 256:(h + 1) * 256],
                                p_k[:, h * 256:(h + 1) * 256],
                                p_kdec[:, j % 2, h:h + 1], None, ALU.mult),
                              inc=(h == H - 1))
                        for bk, pb in enumerate([pMM1, pMM2, pY1, pY2]):
                            for k in range(8):
                                P_(lambda e, k=k, bk=bk, pb=pb, j=j: e.matmul(
                                    pb[...], LP["xnT"][:, k, j * 128:(j + 1) * 128],
                                    b_WvT[:, k, bk * 512:(bk + 1) * 512],
                                    start=(k == 0), stop=(k == 7)),
                                   inc=(k == 7 and bk == 3))
                        w.wait("ve", "PE")
                        for bk, pb in enumerate([pMM1, pMM2, pY1, pY2]):
                            V(lambda e, bk=bk, pb=pb, j=j: e.tensor_copy(
                                p_v[:, j, bk * 512:(bk + 1) * 512], pb[...]),
                              inc=(bk == 3))
                        w.wait("pe", "VE")
                    w.wait("pe", "AC")
                    for cc in range(2):
                        for h in range(H):
                            for dc in range(2):
                                for jj in range(2):
                                    P_(lambda e, cc=cc, h=h, dc=dc, jj=jj: e.matmul(
                                        pXT[:, dc, :],
                                        p_kp[:, cc * 2 + jj,
                                             h * 256 + dc * 128:
                                             h * 256 + (dc + 1) * 128],
                                        p_v[:, cc * 2 + jj, h * 512:(h + 1) * 512],
                                        start=(jj == 0), stop=(jj == 1)),
                                       inc=(jj == 1 and dc == 1))
                            w.wait("ve", "PE")
                            for dc in range(2):
                                V(lambda e, h=h, dc=dc: e.scalar_tensor_tensor(
                                    s_S[:, h, dc, :], s_S[:, h, dc, :], GC[h],
                                    pXT[:, dc, :], ALU.mult, ALU.add), inc=(dc == 1))
                            w.wait("pe", "VE")
                    w.barrier()
                for h in range(H):
                    for dc in range(2):
                        V(lambda e, h=h, dc=dc: e.tensor_copy(s_Sb[:, h, dc, :],
                                                              s_S[:, h, dc, :]),
                          inc=(h == H - 1 and dc == 1))
                w.wait("sy", "VE")
                w.dma("sy", dr_S[...], s_S[...], store=True)
                w.dma("sy", dr_Sb[...], s_Sb[...], store=True)
                w.barrier()

            if "F2" in phases:
                NI = 8 * nblk
                NY = 8 * nblk
                NEH = nq * NE

                def w1src(e, ic, ab):
                    c0 = ab * 1024 + ic * 128
                    return w1p[e][:, c0:c0 + 128].rearrange(
                        "(fc p) x -> p fc x", p=128)

                def w2src(e, hc):
                    return w2d[e][:, hc * 128:(hc + 1) * 128].rearrange(
                        "(ic p) x -> p ic x", p=128)

                for hf in range(chq // 4):
                    w.dma("sy", f_x2q[:, :, hf * 512:(hf + 1) * 512],
                          dr_x2T[hf].rearrange("m p t -> p m t"), sem="X2")
                w.mark("x2_0")
                w.dma("gp", c_row[...], b1p[0].rearrange("(a b) -> a b", a=16),
                      sem="CR")
                w.mark("crow_0")
                for sl in range(2):
                    w.dma("gp", f_w1s[:, :, sl, 0, :], w1src(0, sl, 0),
                          sem=f"W{sl}")
                    w.dma("gp", f_w1s[:, :, sl, 1, :], w1src(0, sl, 1),
                          sem=f"W{sl}")
                    w.mark(f"w1_0_{sl}")
                    w.dma("gp", f_w2s[:, :, sl, :], w2src(0, sl), sem=f"U{sl}")
                    w.mark(f"w2_0_{sl}")
                w.barrier()

                for uid in range(NEH):
                    q, e_i = divmod(uid, NE)
                    cq0 = q * chq
                    last_e = (e_i == NE - 1)
                    w.wmark("pe", "CR", f"crow_{uid}")
                    w.wmark("pe", "VE", f"b1rdy_{uid - 1}")
                    if e_i == 0 and q > 0:
                        w.wmark("pe", "VE", f"hevdone_{q - 1}")
                    P_(lambda e: e.transpose(pXT[:, 0, 0:16], c_row[...],
                                             c_idef[0:16, 0:16]), inc=True)
                    w.mark(f"b1T_{uid}")
                    if uid < NEH - 1:
                        w.wmark("gp", "PE", f"b1T_{uid}")
                        w.dma("gp", c_row[...],
                              b1p[(uid + 1) % NE].rearrange("(a b) -> a b", a=16),
                              sem="CR")
                        w.mark(f"crow_{uid + 1}")
                    w.wmark("ve", "PE", f"b1T_{uid}")
                    w.wmark("ve", "AC", f"evac_{uid - 1}_{NI - 1}")
                    V(lambda e: e.tensor_copy(f_b1T[:, 0:16], pXT[:, 0, 0:16]))
                    V(lambda e: e.tensor_scalar(f_b1T[:, 16:32], pXT[:, 0, 0:16],
                                                0.5, 0.5, ALU.mult, ALU.add),
                      inc=True)
                    w.mark(f"b1rdy_{uid}")
                    for b in range(nblk):
                        gb = pG1[...] if b % 2 == 0 else pXT[:, 0, :]
                        if b < 2:
                            w.wmark("pe", "VE", f"gbc_{uid - 1}_{nblk - 2 + b}")
                        else:
                            w.wmark("pe", "VE", f"gbc_{uid}_{b - 2}")
                        if b % 2 == 1:
                            w.wmark("pe", "VE", f"b1rdy_{uid}")
                        P_(lambda e, e_i=e_i, gb=gb, t0=b * 512, cq0=cq0: e.matmul(
                            gb, c_E16[:, e_i * 128:(e_i + 1) * 128],
                            s_gateT[:, cq0 * 128 + t0:cq0 * 128 + t0 + 512],
                            start=True, stop=True), inc=True)
                        w.mark(f"gbcm_{uid}_{b}")
                        w.wmark("ve", "PE", f"gbcm_{uid}_{b}")
                        w.wmark("ve", "GP", f"s_{uid - 1}_{7 * nblk + b}")
                        V(lambda e, b=b, gb=gb: e.tensor_copy(f_gbc[:, b, :], gb),
                          inc=True)
                        w.mark(f"gbc_{uid}_{b}")
                    for ic in range(8):
                        for b in range(nblk):
                            gi = ic * nblk + b
                            pp = gi % 2
                            pa = pMM1 if pp == 0 else pMM2
                            pbb = pY1 if pp == 0 else pY2
                            if b == 0:
                                w.wmark("pe", f"W{ic % 2}", f"w1_{uid}_{ic}")
                                if e_i == 0 and ic == 0:
                                    w.wmark("pe", "X2", f"x2_{q}")
                            if gi >= 2:
                                w.wmark("pe", "AC", f"evac_{uid}_{gi - 2}")
                            else:
                                w.wmark("pe", "AC", f"evac_{uid - 1}_{NI - 2 + gi}")
                                w.wmark("pe", "VE", f"yTe_{uid - 1}_{NY - 2 + gi}")
                            for k in range(8):
                                P_(lambda e, k=k, ic=ic, pa=pa, t0=b * 512:
                                   e.matmul(pa[...], f_w1s[:, k, ic % 2, 0, :],
                                            f_x2q[:, k, t0:t0 + 512],
                                            start=(k == 0), stop=(k == 7)),
                                   inc=(k == 7))
                            for k in range(8):
                                P_(lambda e, k=k, ic=ic, pbb=pbb, t0=b * 512:
                                   e.matmul(pbb[...], f_w1s[:, k, ic % 2, 1, :],
                                            f_x2q[:, k, t0:t0 + 512],
                                            start=(k == 0), stop=(k == 7)),
                                   inc=(k == 7))
                            w.mark(f"mm_{uid}_{gi}")
                            w.wmark("ac", "PE", f"mm_{uid}_{gi}")
                            if gi >= 2:
                                w.wmark("ac", "GP", f"s_{uid}_{gi - 2}")
                            else:
                                w.wmark("ac", "GP", f"s_{uid - 1}_{NI - 2 + gi}")
                            A(lambda e, ic=ic, pa=pa, pp=pp: e.activation(
                                f_aS[:, pp, :], pa[...], AF.Identity,
                                bias=f_b1T[:, ic:ic + 1]), inc=True)
                            A(lambda e, ic=ic, pbb=pbb, pp=pp: e.activation(
                                f_bS[:, pp, :], pbb[...], AF.Identity, scale=0.5,
                                bias=f_b1T[:, 24 + ic:25 + ic]), inc=True)
                            w.mark(f"evac_{uid}_{gi}")
                            w.wmark("ve", "AC", f"evac_{uid}_{gi}")
                            if gi >= 2:
                                w.wmark("ve", "GP", f"s_{uid}_{gi - 2}")
                            else:
                                w.wmark("ve", "GP", f"s_{uid - 1}_{NI - 2 + gi}")
                            V(lambda e, pp=pp: e.tensor_scalar(
                                f_ac[:, pp, :], f_aS[:, pp, :], LIMIT, None,
                                ALU.min), inc=True)
                            w.mark(f"acmin_{uid}_{gi}")
                            w.wmark("ac", "VE", f"acmin_{uid}_{gi}")
                            A(lambda e, pp=pp: e.activation(
                                f_aS[:, pp, :], f_ac[:, pp, :], AF.Tanh,
                                scale=ALPHA / 2.0), inc=True)
                            w.mark(f"tanh_{uid}_{gi}")
                            V(lambda e, pp=pp: e.tensor_scalar(
                                f_bS[:, pp, :], f_bS[:, pp, :], -3.0, 4.0,
                                ALU.max, ALU.min))
                            w.wmark("ve", "AC", f"tanh_{uid}_{gi}")
                            V(lambda e, pp=pp: e.scalar_tensor_tensor(
                                f_p1[:, pp, :], f_aS[:, pp, :], 1.0,
                                f_ac[:, pp, :], ALU.add, ALU.mult), inc=True)
                            w.mark(f"p1_{uid}_{gi}")
                            if gi == 0:
                                w.wmark("gp", "PE", f"yTm_{uid - 1}_{NY - 1}")
                            w.wmark("gp", "VE", f"p1_{uid}_{gi}")
                            w.wmark("gp", "VE", f"gbc_{uid}_{b}")
                            G(lambda e, pp=pp: e.tensor_tensor(
                                f_ac[:, pp, :], f_p1[:, pp, :], f_bS[:, pp, :],
                                ALU.mult))
                            G(lambda e, ic=ic, pp=pp, b=b: e.tensor_tensor(
                                f_s[:, b, ic, :], f_ac[:, pp, :], f_gbc[:, b, :],
                                ALU.mult), inc=True)
                            w.mark(f"s_{uid}_{gi}")
                        nic = ic + 2
                        ne = uid + (1 if nic >= 8 else 0)
                        if ne < NEH:
                            nic = nic % 8
                            nsl = nic % 2
                            w.wmark("gp", "PE", f"mm_{uid}_{ic * nblk + nblk - 1}")
                            w.dma("gp", f_w1s[:, :, nsl, 0, :],
                                  w1src(ne % NE, nic, 0), sem=f"W{nsl}")
                            w.dma("gp", f_w1s[:, :, nsl, 1, :],
                                  w1src(ne % NE, nic, 1), sem=f"W{nsl}")
                            w.mark(f"w1_{ne}_{nic}")
                        if e_i == NE - 1 and ic == 7 and q + 1 < nq:
                            w.wmark("sy", "PE", f"mm_{uid}_{NI - 1}")
                            for hf in range(chq // 4):
                                w.dma("sy", f_x2q[:, :, hf * 512:(hf + 1) * 512],
                                      dr_x2T[(q + 1) * (chq // 4) + hf]
                                      .rearrange("m p t -> p m t"), sem="X2")
                            w.mark(f"x2_{q + 1}")
                    for hc in range(8):
                        for b in range(nblk):
                            yi = hc * nblk + b
                            pb = pY1 if yi % 2 == 0 else pY2
                            if b == 0:
                                w.wmark("pe", f"U{hc % 2}", f"w2_{uid}_{hc}")
                            if hc == 0:
                                w.wmark("pe", "GP", f"s_{uid}_{7 * nblk + b}")
                            if yi >= 2:
                                w.wmark("pe", "VE", f"yTe_{uid}_{yi - 2}")
                            else:
                                w.wmark("pe", "AC", f"evac_{uid}_{NI - 2 + yi}")
                                w.wmark("pe", "VE", f"yTe_{uid - 1}_{NY - 2 + yi}")
                            for k in range(8):
                                P_(lambda e, k=k, hc=hc, pb=pb, b=b: e.matmul(
                                    pb[...], f_w2s[:, k, hc % 2, :],
                                    f_s[:, b, k, :], start=(k == 0),
                                    stop=(k == 7 and not last_e)),
                                   inc=(k == 7 and not last_e))
                            if last_e:
                                P_(lambda e, hc=hc, pb=pb, t0=b * 512, cq0=cq0:
                                   e.matmul(
                                       pb[...],
                                       c_b2[:, hc * 128:(hc + 1) * 128],
                                       s_gateT[:, cq0 * 128 + t0:
                                               cq0 * 128 + t0 + 512],
                                       start=False, stop=True), inc=True)
                            w.mark(f"yTm_{uid}_{yi}")
                            w.wmark("ve", "PE", f"yTm_{uid}_{yi}")
                            if e_i == 0 and q > 0 and yi == 0:
                                w.wmark("ve", "PE", f"hevT_{q - 1}")
                            V(lambda e, hc=hc, pb=pb, e_i=e_i, b=b: (
                                e.tensor_copy(f_yT[:, b, hc, :], pb[...])
                                if e_i == 0 else
                                e.tensor_tensor(f_yT[:, b, hc, :],
                                                f_yT[:, b, hc, :], pb[...],
                                                ALU.add)), inc=True)
                            w.mark(f"yTe_{uid}_{yi}")
                        nhc = hc + 2
                        ne = uid + (1 if nhc >= 8 else 0)
                        if ne < NEH:
                            nhc = nhc % 8
                            w.wmark("gp", "PE", f"yTm_{uid}_{hc * nblk + nblk - 1}")
                            w.dma("gp", f_w2s[:, :, nhc % 2, :],
                                  w2src(ne % NE, nhc), sem=f"U{nhc % 2}")
                            w.mark(f"w2_{ne}_{nhc}")
                    if last_e:
                        w.wmark("pe", "VE", f"yTe_{uid}_{NY - 1}")
                        for cxl in range(chq):
                            cx = cq0 + cxl
                            b, tc = divmod(cxl, 4)
                            w.wait("sy", "VE")
                            w.dma("sy", f_yst2[:, cxl % 2, :], dr_yacc[cx])
                            for hcc in range(8):
                                pt = pG1 if hcc % 2 == 0 else pXT
                                ptap = (pt[:, 0:128] if hcc % 2 == 0
                                        else pt[:, 0, 0:128])
                                P_(lambda e, b=b, hcc=hcc, tc=tc, ptap=ptap:
                                   e.transpose(
                                       ptap,
                                       f_yT[:, b, hcc, tc * 128:(tc + 1) * 128],
                                       c_idef[...]), inc=True)
                                w.wait("ve", "PE")
                                w.wait("ve", "LD")
                                V(lambda e, hcc=hcc, cxl=cxl, ptap=ptap:
                                  e.tensor_tensor(
                                      f_yst2[:, cxl % 2,
                                             hcc * 128:(hcc + 1) * 128],
                                      f_yst2[:, cxl % 2,
                                             hcc * 128:(hcc + 1) * 128],
                                      ptap, ALU.add), inc=True)
                                w.wait("pe", "VE")
                            w.wait("sy", "VE")
                            w.dma("sy", dr_yacc[cx], f_yst2[:, cxl % 2, :],
                                  store=True)
                        w.mark(f"hevT_{q}")
                        w.mark(f"hevdone_{q}")
                w.barrier()

            if dbg_yacc:
                for c in range(nch):
                    w.dma("sy", s_yst[:, c % 2, :], dr_yacc[c])
                    w.wait("sy", "LD")
                    w.dma("sy", out[c * 128:(c + 1) * 128, :], s_yst[:, c % 2, :],
                          store=True)
                if dbg_dump:
                    w.dma("sy", d_x2[...], dr_x2T[...], store=True)
                    w.dma("sy", d_yT[...], f_yT[...], store=True)
                    w.dma("sy", d_fs[...], f_s[...], store=True)
                    w.dma("sy", d_xq[...], f_x2q[...], store=True)
                w.wait("gp", "ST")
                w.wait("sy", "ST")
                return

            if "F3" in phases:
                w.dma("sy", g_mask[...], maskTi.transpose([1, 0, 2, 3]))
                w.dma("sy", g_kdec[...], kdeci.transpose([1, 0, 2]))
                w.dma("sy", g_qgam[...], qgami[...])
                w.dma("sy", g_S[...], dr_S[...])
                w.dma("sy", g_Sb[...], dr_Sb[...])
                for cc2 in range(16):
                    w.wait("sy", "AC")
                    w.dma("sy", g_wst[...], WoTd[cc2 * 128:(cc2 + 1) * 128, :])
                    w.wait("ac", "LD")
                    A(lambda e, cc2=cc2: e.activation(g_WoT[:, cc2, :], g_wst[...],
                                                      AF.Identity,
                                                      scale=c_gnwT[:, cc2:cc2 + 1]),
                      inc=True)
                w.barrier()
                for s in range(nsb):
                    c0 = 4 * s
                    w.wait("sy", "VE")
                    w.wait("sy", "PE")
                    w.dma("sy", g_qT[...], dr_qT[s].rearrange("m p t -> p m t"))
                    w.dma("sy", g_kT[...], dr_kT[s].rearrange("m p t -> p m t"))
                    w.barrier()
                    for cc in range(2):
                        w.wait("sy", "VE")
                        for jj in range(2):
                            w.dma("sy", g_v[:, jj, :], dr_v[c0 + cc * 2 + jj])
                            w.dma("sy", g_sig[:, jj, :], dr_sig[c0 + cc * 2 + jj])
                        for jj in range(2):
                            w.wait("pe", "AC")
                            for h in range(H):
                                for dc in range(2):
                                    sl = h * 2 + dc
                                    P_(lambda e, h=h, dc=dc, jj=jj, cc=cc, sl=sl:
                                       e.transpose(pTb[:, sl, :],
                                                   g_kT[:, 2 * h + dc,
                                                        cc * 256 + jj * 128:
                                                        cc * 256 + (jj + 1) * 128],
                                                   c_ideb[...]),
                                       inc=(h == H - 1 and dc == 1))
                            w.wait("ac", "PE")
                            for h in range(H):
                                for dc in range(2):
                                    sl = h * 2 + dc
                                    A(lambda e, h=h, dc=dc, jj=jj, sl=sl: e.activation(
                                        g_ktok[:, jj, sl, :], pTb[:, sl, :],
                                        AF.Identity, scale=g_kdec[:, jj, h:h + 1]),
                                      inc=(h == H - 1 and dc == 1))
                        for h in range(H):
                            ob = [pMM1, pMM2, pY1, pY2][h]
                            w.wait("ve", "LD")
                            w.wait("ve", "AC")
                            w.wait("ve", "PE")
                            V(lambda e, h=h, cc=cc: e.tensor_tensor(
                                g_qg[:, 0, :], g_qT[:, 2 * h, cc * 256:(cc + 1) * 256],
                                g_qgam[:, h, :], ALU.mult))
                            V(lambda e, h=h, cc=cc: e.tensor_tensor(
                                g_qg[:, 1, :],
                                g_qT[:, 2 * h + 1, cc * 256:(cc + 1) * 256],
                                g_qgam[:, h, :], ALU.mult), inc=True)
                            w.wait("pe", "VE")
                            for jc in range(2):
                                for dc in range(2):
                                    P_(lambda e, h=h, cc=cc, jc=jc, dc=dc: e.matmul(
                                        pG1[:, jc * 256:(jc + 1) * 256],
                                        g_kT[:, 2 * h + dc,
                                             cc * 256 + jc * 128:
                                             cc * 256 + (jc + 1) * 128],
                                        g_qT[:, 2 * h + dc, cc * 256:(cc + 1) * 256],
                                        start=(dc == 0), stop=(dc == 1)),
                                       inc=(jc == 1 and dc == 1))
                            w.wait("ve", "PE")
                            for jc in range(2):
                                V(lambda e, h=h, jc=jc: e.tensor_tensor(
                                    g_am[:, h, jc, :], pG1[:, jc * 256:(jc + 1) * 256],
                                    g_mask[:, jc, h, :], ALU.mult), inc=(jc == 1))
                            w.wait("pe", "VE")
                            for qq in range(2):
                                for jc in range(2):
                                    P_(lambda e, h=h, qq=qq, jc=jc, ob=ob: e.matmul(
                                        ob[...],
                                        g_am[:, h, jc, qq * 128:(qq + 1) * 128],
                                        g_v[:, jc, h * 512:(h + 1) * 512],
                                        start=(jc == 0), stop=False))
                                for dc in range(2):
                                    P_(lambda e, h=h, qq=qq, dc=dc, ob=ob: e.matmul(
                                        ob[...], g_qg[:, dc, qq * 128:(qq + 1) * 128],
                                        g_Sb[:, h, dc, :], start=False,
                                        stop=(dc == 1)), inc=(dc == 1))
                                w.wait("ac", "PE")
                                w.wait("ac", "VE")
                                w.wait("ac", "AC")
                                A(lambda e, h=h, qq=qq, ob=ob: e.activation(
                                    g_junk[:, h, :], ob[...], AF.Square,
                                    accum_out=g_ssq[:, 4 * qq + h:4 * qq + h + 1]),
                                  inc=True)
                                A(lambda e, h=h, qq=qq, ob=ob: e.activation(
                                    g_osc[:, qq, h * 512:(h + 1) * 512], ob[...],
                                    AF.Identity), inc=True)
                                w.wait("pe", "AC")
                        for h in range(H):
                            for dc in range(2):
                                for jj in range(2):
                                    P_(lambda e, h=h, dc=dc, jj=jj: e.matmul(
                                        pXT[:, dc, :], g_ktok[:, jj, h * 2 + dc, :],
                                        g_v[:, jj, h * 512:(h + 1) * 512],
                                        start=(jj == 0), stop=(jj == 1)),
                                       inc=(jj == 1 and dc == 1))
                            w.wait("ve", "PE")
                            for dc in range(2):
                                V(lambda e, h=h, dc=dc: e.scalar_tensor_tensor(
                                    g_S[:, h, dc, :], g_S[:, h, dc, :], GC[h],
                                    pXT[:, dc, :], ALU.mult, ALU.add))
                            for dc in range(2):
                                V(lambda e, h=h, dc=dc: e.tensor_copy(
                                    g_Sb[:, h, dc, :], g_S[:, h, dc, :]),
                                  inc=(dc == 1))
                            w.wait("pe", "VE")
                        w.wait("ve", "AC")
                        if dbg_dump:
                            w.wait("sy", "AC")
                            ccx = s * 2 + cc
                            w.dma("sy", d_osc[c0 + cc * 2], g_osc[:, 0, :], store=True)
                            w.dma("sy", d_osc[c0 + cc * 2 + 1], g_osc[:, 1, :], store=True)
                            w.dma("sy", d_ssq[ccx], g_ssq[...], store=True)
                        w.vc(lambda e: e.tensor_scalar(r_a[...], g_ssq[...], 1.0 / DV,
                                                       EPS, ALU.mult, ALU.add))
                        rsqrt_chain()
                        V(lambda e: e.tensor_copy(r_t[...], r_y[...]), inc=True)
                        if dbg_dump:
                            w.wait("sy", "VE")
                            w.dma("sy", d_ry[s * 2 + cc], r_y[...], store=True)
                        for qq in range(2):
                            c = c0 + cc * 2 + qq
                            w.dma("sy", g_yre[:, qq, :], dr_yacc[c])
                            for h in range(H):
                                w.wait("ac", "VE")
                                A(lambda e, qq=qq, h=h: e.activation(
                                    g_wst[:, 0:512],
                                    g_osc[:, qq, h * 512:(h + 1) * 512], AF.Identity,
                                    scale=r_y[:, 4 * qq + h:4 * qq + h + 1]),
                                  inc=True)
                                w.wait("ve", "AC")
                                V(lambda e, qq=qq, h=h: e.tensor_tensor(
                                    g_onb[:, h * 512:(h + 1) * 512], g_wst[:, 0:512],
                                    g_sig[:, qq, h * 512:(h + 1) * 512], ALU.mult),
                                  inc=True)
                            w.wait("pe", "VE")
                            for half8 in range(2):
                                for t8 in range(8):
                                    cc2 = half8 * 8 + t8
                                    P_(lambda e, cc2=cc2, t8=t8: e.transpose(
                                        pTb[:, t8, :],
                                        g_onb[:, cc2 * 128:(cc2 + 1) * 128],
                                        c_ideb[...]), inc=(t8 == 7))
                                w.wait("ve", "PE")
                                V(lambda e, half8=half8: e.tensor_copy(
                                    g_onT[:, half8 * 8:(half8 + 1) * 8, :], pTb[...]),
                                  inc=True)
                                w.wait("pe", "VE")
                            for nb in range(2):
                                pb = pG1 if nb == 0 else pXT
                                pbap = pb[...] if nb == 0 else pb[:, 0, :]
                                for cc2 in range(16):
                                    P_(lambda e, nb=nb, cc2=cc2, pbap=pbap: e.matmul(
                                        pbap, g_onT[:, cc2, :],
                                        g_WoT[:, cc2, nb * 512:(nb + 1) * 512],
                                        start=(cc2 == 0), stop=(cc2 == 15)),
                                       inc=(cc2 == 15))
                            w.wait("ve", "PE")
                            w.wait("ve", "LD")
                            w.wait("ve", "ST")
                            V(lambda e, qq=qq: e.tensor_tensor(
                                g_fin[:, qq, 0:512], g_yre[:, qq, 0:512], pG1[...],
                                ALU.add))
                            V(lambda e, qq=qq: e.tensor_tensor(
                                g_fin[:, qq, 512:1024], g_yre[:, qq, 512:1024],
                                pXT[:, 0, :], ALU.add), inc=True)
                            w.wait("sy", "VE")
                            w.dma("sy", out[c * 128:(c + 1) * 128, :], g_fin[:, qq, :],
                                  store=True)
                            w.wait("pe", "VE")
                    w.barrier()

            if dbg_dump:
                w.dma("sy", d_v[...], dr_v[...], store=True)
                w.dma("sy", d_sig[...], dr_sig[...], store=True)
                w.dma("sy", d_qT[...], dr_qT[...], store=True)
                w.dma("sy", d_kT[...], dr_kT[...], store=True)
                w.dma("sy", d_S[...], g_S[...], store=True)
                w.dma("sy", d_kp[...], p_kp[...], store=True)
                w.dma("sy", d_pk[...], p_k[...], store=True)
                w.dma("sy", d_pv[...], p_v[...], store=True)
            w.wait("gp", "ST")
            w.wait("sy", "ST")

        @blk.sync
        def _(eng):
            walk(Walk("sy", eng))

        @blk.gpsimd
        def _(eng):
            walk(Walk("gp", eng))

        @blk.tensor
        def _(eng):
            walk(Walk("pe", eng))

        @blk.vector
        def _(eng):
            walk(Walk("ve", eng))

        @blk.scalar
        def _(eng):
            walk(Walk("ac", eng))

    nc.finalize()
    return nc



def _host_tables(T, pos0):
    pos = np.arange(pos0, pos0 + T, dtype=np.float64)
    invf = 1.0 / (10000.0 ** (np.arange(0, DK, 2, dtype=np.float64) / DK))
    fr = pos[:, None] * invf[None, :]
    cos = np.cos(fr)
    sin = np.sin(fr)
    sc = 1.0 / np.sqrt(DK)
    return (np.ascontiguousarray((cos.T * sc).astype(bfl)),
            np.ascontiguousarray((sin.T * sc).astype(bfl)),
            np.ascontiguousarray(cos.T.astype(bfl)),
            np.ascontiguousarray(sin.T.astype(bfl)),
            np.ascontiguousarray(cos.astype(bfl)),
            np.ascontiguousarray(sin.astype(bfl)))


def _host_consts():
    consts = {}
    consts["ones1"] = np.ones((1, 128), np.float32)
    consts["ide_b"] = np.eye(128, dtype=bfl)
    consts["ide_f"] = np.eye(128, dtype=np.float32)
    e16 = np.zeros((16, 2048), np.float32)
    for e in range(16):
        e16[e, e * 128:(e + 1) * 128] = 1.0
    consts["E16t"] = e16
    j = np.arange(CP)
    i = np.arange(CP)
    diff = i[None, :] - j[:, None]
    mask = np.zeros((2, 128, H, CP), np.float64)
    for h in range(H):
        m = np.where(diff >= 0, np.float64(GAMMA[h]) ** np.maximum(diff, 0), 0.0)
        mask[0, :, h, :] = m[0:128]
        mask[1, :, h, :] = m[128:256]
    consts["maskTi"] = mask.astype(bfl)
    kd = np.zeros((2, 128, H), np.float64)
    for h in range(H):
        v = np.float64(GAMMA[h]) ** (CP - 1.0 - j)
        kd[0, :, h] = v[0:128]
        kd[1, :, h] = v[128:256]
    consts["kdeci"] = kd.astype(np.float32)
    qg = np.zeros((128, H, CP), np.float64)
    for h in range(H):
        qg[:, h, :] = (np.float64(GAMMA[h]) ** (i + 1.0))[None, :]
    consts["qgami"] = qg.astype(bfl)
    return consts


def kernel(L0_attn_x, L0_attn_Wq, L0_attn_Wk, L0_attn_Wv, L0_attn_Wg, L0_attn_Wo,
           L0_attn_gnorm_w, L0_ffn_norm_scale, L0_ffn_wg, L0_ffn_bg, L0_ffn_w1,
           L0_ffn_b1, L0_ffn_w2, L0_ffn_b2):
    x = np.asarray(L0_attn_x, np.float32)
    B, TT, _ = x.shape
    Th = TT // 2
    nch = Th // 128

    perm = np.concatenate([np.arange(0, 2048, 2), np.arange(1, 2048, 2)])
    common = {
        "WqT": np.ascontiguousarray(np.asarray(L0_attn_Wq, np.float32).T),
        "WkT": np.ascontiguousarray(np.asarray(L0_attn_Wk, np.float32).T),
        "WvT": np.ascontiguousarray(np.asarray(L0_attn_Wv, np.float32).T),
        "WgT": np.ascontiguousarray(np.asarray(L0_attn_Wg, np.float32).T),
        "WoTd": np.ascontiguousarray(np.asarray(L0_attn_Wo, np.float32).T),
        "w1p": np.ascontiguousarray(np.asarray(L0_ffn_w1, np.float32)[:, :, perm]),
        "w2d": np.ascontiguousarray(np.asarray(L0_ffn_w2, np.float32)),
        "b1p": np.ascontiguousarray(np.asarray(L0_ffn_b1, np.float32)[:, perm]),
        "b2d": np.ascontiguousarray(np.asarray(L0_ffn_b2, np.float32)),
        "wgp": np.ascontiguousarray(
            np.concatenate([np.asarray(L0_ffn_wg, np.float32),
                            np.zeros((D, 16), np.float32)], axis=1)),
        "bg32": np.ascontiguousarray(
            np.concatenate([np.asarray(L0_ffn_bg, np.float32).reshape(1, 16),
                            np.full((1, 16), NEG, np.float32)], axis=1)),
        "gnw16": np.ascontiguousarray(
            np.tile(np.asarray(L0_attn_gnorm_w, np.float32), H).reshape(16, 128)),
        "ns_r": np.ascontiguousarray(
            np.asarray(L0_ffn_norm_scale, np.float32).reshape(1, D)),
    }
    common.update(_host_consts())
    cq0, sq0, cf0, sf0, ctm, stm = _host_tables(Th, 0)
    cq1, sq1, cf1, sf1, _, _ = _host_tables(Th, Th)

    nc = build_program(nch=nch)
    in_maps = []
    zeros_pre = np.zeros((Th, D), np.float32)
    for core in range(8):
        b, half = core // 2, core % 2
        m = dict(common)
        m["x_own"] = np.ascontiguousarray(x[b, half * Th:(half + 1) * Th])
        m["x_pre"] = np.ascontiguousarray(x[b, 0:Th]) if half else zeros_pre
        if half:
            m["cosq"], m["sinq"], m["cosf"], m["sinf"] = cq1, sq1, cf1, sf1
        else:
            m["cosq"], m["sinq"], m["cosf"], m["sinf"] = cq0, sq0, cf0, sf0
        m["costm"], m["sintm"] = ctm, stm
        in_maps.append(m)

    res = run_bass_kernel_spmd(nc, in_maps, core_ids=list(range(8)))
    outp = np.empty((B, TT, D), np.float32)
    for core in range(8):
        b, half = core // 2, core % 2
        outp[b, half * Th:(half + 1) * Th] = res.results[core]["out"]
    return outp


# version 10 (final)
# speedup vs baseline: 1.2766x; 1.0165x over previous
import numpy as np
import ml_dtypes
from contextlib import ExitStack

import concourse.bass as bass
import concourse.mybir as mybir
from concourse.bass_utils import run_bass_kernel_spmd

F32 = mybir.dt.float32
BF16 = mybir.dt.bfloat16
I32 = mybir.dt.int32
AF = mybir.ActivationFunctionType
ALU = mybir.AluOpType
AX = mybir.AxisListType
bfl = ml_dtypes.bfloat16
SYNC_LOG = None
F2_PREFETCH = True

D = 1024
H = 4
DK = 256
DV = 512
VD = 2048
CP = 256
NE = 16
EPS = 1e-5
ALPHA = 1.702
LIMIT = 7.0
GAMMA = [1.0 - 2.0 ** (-5 - h) for h in range(H)]
GC = [float(np.float64(g) ** CP) for g in GAMMA]
MAGICF = float(0x5F3759DF)
NEG = -1e30


def build_program(nch=32, phases=("F1a", "F1b", "P", "F2", "F3"), dbg_yacc=False, dbg_dump=False):
    T = nch * 128
    nsb = nch // 4
    chq = min(16, nch)
    nq = nch // chq
    nblk = chq // 4

    nc = bass.Bass("TRN2", target_bir_lowering=False)

    def din(name, shape, dt):
        return nc.dram_tensor(name, list(shape), dt, kind="ExternalInput").ap()

    x_own = din("x_own", (T, D), F32)
    x_pre = din("x_pre", (T, D), F32)
    WqTd = din("WqT", (D, 1024), F32)
    WkTd = din("WkT", (D, 1024), F32)
    WvTd = din("WvT", (D, VD), F32)
    WgTd = din("WgT", (D, VD), F32)
    WoTd = din("WoTd", (VD, D), F32)
    w1p = din("w1p", (NE, D, 2048), F32)
    w2d = din("w2d", (NE, 1024, D), F32)
    b1p = din("b1p", (NE, 2048), F32)
    b2d = din("b2d", (NE, D), F32)
    wgp = din("wgp", (D, 32), F32)
    bg32 = din("bg32", (1, 32), F32)
    gnw16 = din("gnw16", (16, 128), F32)
    ns_r = din("ns_r", (1, D), F32)
    ones1 = din("ones1", (1, 128), F32)
    ide_b = din("ide_b", (128, 128), BF16)
    ide_f = din("ide_f", (128, 128), F32)
    E16t = din("E16t", (16, 2048), F32)
    cosq = din("cosq", (128, T), BF16)
    sinq = din("sinq", (128, T), BF16)
    cosf = din("cosf", (128, T), BF16)
    sinf = din("sinf", (128, T), BF16)
    costm = din("costm", (T, 128), BF16)
    sintm = din("sintm", (T, 128), BF16)
    maskTi = din("maskTi", (2, 128, H, CP), BF16)
    kdeci = din("kdeci", (2, 128, H), F32)
    qgami = din("qgami", (128, H, CP), BF16)

    out = nc.dram_tensor("out", [T, D], F32, kind="ExternalOutput").ap()
    if dbg_dump:
        d_v = nc.dram_tensor("d_v", [nch, 128, VD], BF16, kind="ExternalOutput").ap()
        d_sig = nc.dram_tensor("d_sig", [nch, 128, VD], BF16, kind="ExternalOutput").ap()
        d_qT = nc.dram_tensor("d_qT", [nch // 4, 8, 128, 512], BF16, kind="ExternalOutput").ap()
        d_kT = nc.dram_tensor("d_kT", [nch // 4, 8, 128, 512], BF16, kind="ExternalOutput").ap()
        d_S = nc.dram_tensor("d_S", [128, H, 2, 512], F32, kind="ExternalOutput").ap()
        d_kp = nc.dram_tensor("d_kp", [128, 4, 1024], BF16, kind="ExternalOutput").ap()
        d_pk = nc.dram_tensor("d_pk", [128, 1024], F32, kind="ExternalOutput").ap()
        d_osc = nc.dram_tensor("d_osc", [nch, 128, VD], F32, kind="ExternalOutput").ap()
        d_x2 = nc.dram_tensor("d_x2", [nch // 4, 8, 128, 512], BF16, kind="ExternalOutput").ap()
        d_yT = nc.dram_tensor("d_yT", [128, chq // 4, 8, 512], F32, kind="ExternalOutput").ap()
        d_fs = nc.dram_tensor("d_fs", [128, chq // 4, 8, 512], BF16, kind="ExternalOutput").ap()
        d_xq = nc.dram_tensor("d_xq", [128, 8, chq * 128], BF16, kind="ExternalOutput").ap()
        d_ssq = nc.dram_tensor("d_ssq", [nch // 2, 128, 32], F32, kind="ExternalOutput").ap()
        d_ry = nc.dram_tensor("d_ry", [nch // 2, 128, 32], F32, kind="ExternalOutput").ap()
        d_pv = nc.dram_tensor("d_pv", [128, 4, VD], BF16, kind="ExternalOutput").ap()

    def dscr(name, shape, dt):
        return nc.dram_tensor(name, list(shape), dt, kind="Internal").ap()

    dr_xnT = dscr("dr_xnT", (nsb, 8, 128, 512), BF16)
    dr_qT = dscr("dr_qT", (nsb, 8, 128, 512), BF16)
    dr_kT = dscr("dr_kT", (nsb, 8, 128, 512), BF16)
    dr_v = dscr("dr_v", (nch, 128, VD), BF16)
    dr_sig = dscr("dr_sig", (nch, 128, VD), BF16)
    dr_x2T = dscr("dr_x2T", (nsb, 8, 128, 512), BF16)
    dr_yacc = dscr("dr_yacc", (nch, 128, D), F32)
    dr_S = dscr("dr_S", (128, H, 2, 512), F32)
    dr_Sb = dscr("dr_Sb", (128, H, 2, 512), BF16)

    AL = lambda n, s, d: nc.alloc_sbuf_tensor(n, list(s), d)

    c_ideb = AL("c_ideb", (128, 128), BF16)
    c_idef = AL("c_idef", (128, 128), F32)
    c_ones = AL("c_ones", (1, 128), F32)
    c_E16 = AL("c_E16", (16, 2048), F32)
    c_gnwT = AL("c_gnwT", (128, 16), F32)
    c_b2 = AL("c_b2", (16, D), F32)
    c_row = AL("c_row", (16, 128), F32)
    s_gateT = AL("s_gateT", (16, T), F32)
    r_a = AL("r_a", (128, 32), F32)
    r_sh = AL("r_sh", (128, 32), I32)
    r_shf = AL("r_shf", (128, 32), F32)
    r_zf = AL("r_zf", (128, 32), F32)
    r_zi = AL("r_zi", (128, 32), I32)
    r_y = AL("r_y", (128, 32), F32)
    r_t = AL("r_t", (128, 32), F32)
    r_y2 = AL("r_y2", (128, 32), F32)

    mark = nc.sbuf_base
    b_WkT = AL("b_WkT", (128, 8, 1024), BF16)
    b_WvT = AL("b_WvT", (128, 8, VD), BF16)
    mark2 = nc.sbuf_base

    def ln_group(tag):
        return dict(
            x4=AL(tag + "x4", (128, 4, D), F32),
            junk=AL(tag + "junk", (128, 4, D), F32),
            S1=AL(tag + "S1", (128, 32), F32),
            S2=AL(tag + "S2", (128, 32), F32),
            mean=AL(tag + "mean", (128, 32), F32),
            msq=AL(tag + "msq", (128, 32), F32),
            var=AL(tag + "var", (128, 32), F32),
            nmrs=AL(tag + "nmrs", (128, 32), F32),
            u=AL(tag + "u", (128, 32), F32),
            xn=AL(tag + "xn", (128, D), F32),
            xnb=AL(tag + "xnb", (128, D), BF16),
            xnT=AL(tag + "xnT", (128, 8, 512), BF16),
        )

    nc.sbuf_base = mark2
    L1 = ln_group("a_")
    c_wg = AL("c_wg", (128, 8, 32), F32)
    c_nsbc = AL("c_nsbc", (128, D), F32)
    c_bgbc = AL("c_bgbc", (128, 32), F32)
    s_x2 = AL("s_x2", (128, D), F32)
    s_x2s = AL("s_x2s", (128, D), F32)
    s_x2Tf = AL("s_x2Tf", (128, 2, 4, 128), F32)
    s_x2Tb = AL("s_x2Tb", (128, 8, 512), BF16)
    s_yst = AL("s_yst", (128, 2, D), F32)
    s_log = AL("s_log", (128, 4, 32), F32)
    s_mx = AL("s_mx", (128, 32), F32)
    s_hb = AL("s_hb", (128, 32), F32)
    s_th = AL("s_th", (128, 4, 32), F32)
    s_ea = AL("s_ea", (128, 4, 32), F32)
    s_eb = AL("s_eb", (128, 4, 32), F32)
    s_es = AL("s_es", (128, 4, 32), F32)
    s_sel = AL("s_sel", (128, 4, 32), F32)
    s_den = AL("s_den", (128, 32), F32)
    s_rden = AL("s_rden", (128, 32), F32)
    s_g16 = AL("s_g16", (128, 32), F32)
    assert nc.sbuf_base <= nc.sbuf_top

    nc.sbuf_base = mark2
    b_WqT = AL("b_WqT", (128, 8, 1024), BF16)
    b_WgT = AL("b_WgT", (128, 8, VD), BF16)
    b_xnT = AL("b_xnT", (128, 2, 8, 512), BF16)
    b_qT = AL("b_qT", (128, 8, 512), BF16)
    b_kT = AL("b_kT", (128, 8, 512), BF16)
    b_rt1 = AL("b_rt1", (128, 512), F32)
    b_rt2 = AL("b_rt2", (128, 512), F32)
    b_cq = AL("b_cq", (128, 2, 512), BF16)
    b_sq = AL("b_sq", (128, 2, 512), BF16)
    b_ck = AL("b_ck", (128, 2, 512), BF16)
    b_sk = AL("b_sk", (128, 2, 512), BF16)
    b_v = AL("b_v", (128, VD), BF16)
    b_sg = AL("b_sg", (128, VD), BF16)
    assert nc.sbuf_base <= nc.sbuf_top

    nc.sbuf_base = mark2
    LP = ln_group("p_")
    p_k = AL("p_k", (128, 1024), F32)
    p_kp = AL("p_kp", (128, 4, 1024), BF16)
    p_v = AL("p_v", (128, 4, VD), BF16)
    p_ct = AL("p_ct", (128, 128), BF16)
    p_st = AL("p_st", (128, 128), BF16)
    p_rt1 = AL("p_rt1", (128, 512), F32)
    p_rt2 = AL("p_rt2", (128, 512), F32)
    p_kdec = AL("p_kdec", (128, 2, H), F32)
    s_S = AL("s_S", (128, H, 2, 512), F32)
    s_Sb = AL("s_Sb", (128, H, 2, 512), BF16)
    assert nc.sbuf_base <= nc.sbuf_top

    nc.sbuf_base = mark
    f_w1s = AL("f_w1s", (128, 8, 3, 2, 128), BF16)
    f_w2s = AL("f_w2s", (128, 8, 2, 128), BF16)
    f_b1T = AL("f_b1T", (128, 32), F32)
    f_yst2 = AL("f_yst2", (128, 2, D), F32)
    f_x2q = AL("f_x2q", (128, 8, chq * 128), BF16)
    f_gbc = AL("f_gbc", (128, nblk, 512), F32)
    f_aS = AL("f_aS", (128, 2, 512), F32)
    f_bS = AL("f_bS", (128, 2, 512), F32)
    f_ac = AL("f_ac", (128, 2, 512), F32)
    f_p1 = AL("f_p1", (128, 2, 512), F32)
    f_s = AL("f_s", (128, nblk, 8, 512), BF16)
    f_yT = AL("f_yT", (128, nblk, 8, 512), F32)
    assert nc.sbuf_base <= nc.sbuf_top

    nc.sbuf_base = mark
    g_mask = AL("g_mask", (128, 2, H, CP), BF16)
    g_kdec = AL("g_kdec", (128, 2, H), F32)
    g_qgam = AL("g_qgam", (128, H, CP), BF16)
    g_S = AL("g_S", (128, H, 2, 512), F32)
    g_Sb = AL("g_Sb", (128, H, 2, 512), BF16)
    g_qT = AL("g_qT", (128, 8, 512), BF16)
    g_kT = AL("g_kT", (128, 8, 512), BF16)
    g_v = AL("g_v", (128, 2, VD), BF16)
    g_sig = AL("g_sig", (128, 2, VD), BF16)
    g_ktok = AL("g_ktok", (128, 2, 8, 128), BF16)
    g_qg = AL("g_qg", (128, 2, 256), BF16)
    g_am = AL("g_am", (128, H, 2, 256), BF16)
    g_osc = AL("g_osc", (128, 2, VD), F32)
    g_ssq = AL("g_ssq", (128, 32), F32)
    g_wst = AL("g_wst", (128, D), F32)
    g_onb = AL("g_onb", (128, VD), BF16)
    g_onT = AL("g_onT", (128, 16, 128), BF16)
    g_WoT = AL("g_WoT", (128, 16, D), BF16)
    g_yre = AL("g_yre", (128, 2, D), F32)
    g_fin = AL("g_fin", (128, 2, D), F32)
    g_junk = AL("g_junk", (128, 4, 512), F32)
    assert nc.sbuf_base <= nc.sbuf_top

    ctx = ExitStack()
    with ctx:
        pMM1 = ctx.enter_context(nc.psum_tensor("pMM1", [128, 512], F32))
        pMM2 = ctx.enter_context(nc.psum_tensor("pMM2", [128, 512], F32))
        pY1 = ctx.enter_context(nc.psum_tensor("pY1", [128, 512], F32))
        pY2 = ctx.enter_context(nc.psum_tensor("pY2", [128, 512], F32))
        pG1 = ctx.enter_context(nc.psum_tensor("pG1", [128, 512], F32))
        pXT = ctx.enter_context(nc.psum_tensor("pXT", [128, 2, 512], F32))
        pTb = ctx.enter_context(nc.psum_tensor("pTb", [128, 8, 128], BF16))

        sems = {s: ctx.enter_context(nc.semaphore(s))
                for s in ("LD", "ST", "GP", "PE", "VE", "AC",
                          "W0", "W1", "W2", "U0", "U1", "CR", "X2")}

        blk = ctx.enter_context(nc.Block())

        class Walk:
            def __init__(self, E, eng):
                self.E = E
                self.eng = eng
                self.c = {k: 0 for k in sems}
                self.mk = {}

            def _emit(self, engine, sem, n, builder):
                if sem is not None:
                    self.c[sem] += n
                if self.E == engine:
                    if SYNC_LOG is not None:
                        SYNC_LOG.append((self.E, "inc" if sem else "op", sem, n))
                    else:
                        ins = builder(self.eng)
                        if sem is not None:
                            ins.then_inc(sems[sem], n)

            def pe(self, b, inc=False):
                self._emit("pe", "PE" if inc else None, 1, b)

            def ve(self, b, inc=False):
                self._emit("ve", "VE" if inc else None, 1, b)

            def vc(self, b):
                self._emit("ve", "VE", 1, b)
                self.wait("ve", "VE")

            def ac(self, b, inc=False):
                self._emit("ac", "AC" if inc else None, 1, b)

            def gp(self, b, inc=False):
                self._emit("gp", "GP" if inc else None, 1, b)

            def dma(self, q, outap, inap, store=False, sem=None):
                if sem is None:
                    sem = "GP" if q == "gp" else ("ST" if store else "LD")
                self._emit(q, sem, 16, lambda e: e.dma_start(out=outap, in_=inap))

            def wait(self, engine, sem, val=None):
                if val is None:
                    val = self.c[sem]
                if self.E == engine:
                    if SYNC_LOG is not None:
                        SYNC_LOG.append((self.E, "wait", sem, val))
                    else:
                        self.eng.wait_ge(sems[sem], val)

            def mark(self, name):
                self.mk[name] = dict(self.c)

            def wmark(self, engine, sem, name):
                if name in self.mk:
                    self.wait(engine, sem, self.mk[name][sem])

            def barrier(self):
                for e in ("sy", "gp", "pe", "ve", "ac"):
                    for s in ("LD", "ST", "GP", "PE", "VE", "AC",
                              "W0", "W1", "W2", "U0", "U1", "CR", "X2"):
                        self.wait(e, s)

        def walk(w):
            V, P_, A, G = w.ve, w.pe, w.ac, w.gp

            w.dma("sy", c_ideb[...], ide_b[...])
            w.dma("sy", c_idef[...], ide_f[...])
            w.dma("sy", c_ones[...], ones1[...])
            w.dma("sy", c_E16[...], E16t[...])
            w.dma("sy", c_b2[...], b2d[...])
            w.dma("sy", c_row[...], gnw16[...])
            w.wait("pe", "LD")
            P_(lambda e: e.transpose(pXT[:, 0, 0:16], c_row[...], c_idef[0:16, 0:16]),
               inc=True)
            w.wait("ve", "PE")
            V(lambda e: e.tensor_copy(c_gnwT[...], pXT[:, 0, 0:16]), inc=True)
            w.barrier()

            def rsqrt_chain():
                Vc = w.vc
                Vc(lambda e: e.tensor_single_scalar(r_sh[...], r_a[...].bitcast(I32), 1,
                                                    ALU.arith_shift_right))
                w.wait("ac", "VE")
                A(lambda e: e.activation(r_shf[...], r_sh[...], AF.Identity), inc=True)
                w.wait("ve", "AC")
                Vc(lambda e: e.tensor_scalar(r_zf[...], r_shf[...], -1.0, MAGICF,
                                             ALU.mult, ALU.add))
                w.wait("ac", "VE")
                A(lambda e: e.activation(r_zi[...], r_zf[...], AF.Identity), inc=True)
                w.wait("ve", "AC")
                Vc(lambda e: e.tensor_copy(r_y[...], r_zi[...].bitcast(F32)))
                for _ in range(3):
                    Vc(lambda e: e.tensor_tensor(r_t[...], r_y[...], r_y[...], ALU.mult))
                    Vc(lambda e: e.tensor_tensor(r_t[...], r_t[...], r_a[...], ALU.mult))
                    Vc(lambda e: e.tensor_scalar(r_t[...], r_t[...], -0.5, 1.5,
                                                 ALU.mult, ALU.add))
                    Vc(lambda e: e.tensor_tensor(r_y[...], r_y[...], r_t[...], ALU.mult))

            def ln_stats(L, rms=True):
                w.wait("ac", "LD")
                w.wait("ve", "LD")
                w.wait("ac", "AC")
                for j in range(4):
                    A(lambda e, j=j: e.activation(L["junk"][:, j, :],
                                                  L["x4"][:, j, :],
                                                  AF.Square,
                                                  accum_out=L["S2"][:, j:j + 1]),
                      inc=(j == 3))
                for j in range(4):
                    V(lambda e, j=j: e.reduce_sum(L["S1"][:, j:j + 1], L["x4"][:, j, :],
                                                  axis=AX.X), inc=(j == 3))
                w.wait("ve", "VE")
                w.wait("ve", "AC")
                Vc = w.vc
                Vc(lambda e: e.tensor_scalar(L["mean"][...], L["S1"][...], 1.0 / D, None,
                                             ALU.mult))
                Vc(lambda e: e.tensor_scalar(L["msq"][...], L["S2"][...], 1.0 / D, None,
                                             ALU.mult))
                Vc(lambda e: e.tensor_tensor(L["var"][...], L["mean"][...],
                                             L["mean"][...], ALU.mult))
                Vc(lambda e: e.tensor_tensor(L["var"][...], L["msq"][...], L["var"][...],
                                             ALU.subtract))
                Vc(lambda e: e.tensor_scalar(r_a[...], L["var"][...], EPS, None, ALU.add))
                rsqrt_chain()
                Vc(lambda e: e.tensor_tensor(L["nmrs"][...], L["mean"][...], r_y[...],
                                             ALU.mult))
                Vc(lambda e: e.tensor_scalar(L["nmrs"][...], L["nmrs"][...], -1.0, None,
                                             ALU.mult))
                if rms:
                    Vc(lambda e: e.tensor_tensor(L["u"][...], r_y[...], r_y[...],
                                                 ALU.mult))
                    Vc(lambda e: e.tensor_scalar(L["u"][...], L["u"][...], -EPS,
                                                 1.0 + EPS, ALU.mult, ALU.add))
                    Vc(lambda e: e.tensor_scalar(r_y2[...], L["u"][...], -0.5, 1.5,
                                                 ALU.mult, ALU.add))
                    for _ in range(2):
                        Vc(lambda e: e.tensor_tensor(r_t[...], r_y2[...], r_y2[...],
                                                     ALU.mult))
                        Vc(lambda e: e.tensor_tensor(r_t[...], r_t[...], L["u"][...],
                                                     ALU.mult))
                        Vc(lambda e: e.tensor_scalar(r_t[...], r_t[...], -0.5, 1.5,
                                                     ALU.mult, ALU.add))
                        Vc(lambda e: e.tensor_tensor(r_y2[...], r_y2[...], r_t[...],
                                                     ALU.mult))
                V(lambda e: e.tensor_copy(r_t[...], r_y[...]), inc=True)

            def normalize_chunk(L, j):
                w.wait("ac", "VE")
                A(lambda e, j=j: e.activation(L["xn"][...], L["x4"][:, j, :],
                                              AF.Identity, scale=r_y[:, j:j + 1],
                                              bias=L["nmrs"][:, j:j + 1]), inc=True)
                w.wait("ve", "AC")
                V(lambda e, L=L: e.tensor_copy(L["xnb"][...], L["xn"][...]), inc=True)
                w.wait("pe", "VE")
                for fc in range(8):
                    P_(lambda e, fc=fc, L=L: e.transpose(
                        pTb[:, fc, :], L["xnb"][:, fc * 128:(fc + 1) * 128],
                        c_ideb[...]), inc=(fc == 7))
                w.wait("ve", "PE")
                V(lambda e, j=j, L=L: e.tensor_copy(
                    L["xnT"][:, :, j * 128:(j + 1) * 128], pTb[...]), inc=True)

            if "F1a" in phases:
                w.dma("sy", c_wg[...], wgp.rearrange("(fc p) e -> p fc e", p=128))
                w.dma("sy", c_nsbc[0:1, :], ns_r[...])
                w.dma("sy", c_bgbc[0:1, :], bg32[...])
                w.wait("pe", "LD")
                P_(lambda e: e.matmul(pMM1[...], c_ones[...], c_nsbc[0:1, 0:512],
                                      start=True, stop=True))
                P_(lambda e: e.matmul(pMM2[...], c_ones[...], c_nsbc[0:1, 512:1024],
                                      start=True, stop=True))
                P_(lambda e: e.matmul(pY1[:, 0:32], c_ones[...], c_bgbc[0:1, 0:32],
                                      start=True, stop=True), inc=True)
                w.wait("ve", "PE")
                V(lambda e: e.tensor_copy(c_nsbc[:, 0:512], pMM1[...]))
                V(lambda e: e.tensor_copy(c_nsbc[:, 512:1024], pMM2[...]))
                V(lambda e: e.tensor_copy(c_bgbc[...], pY1[:, 0:32]))
                V(lambda e: e.memset(L1["S1"][...], 0.0))
                V(lambda e: e.memset(L1["S2"][...], 0.0))
                V(lambda e: e.memset(s_mx[...], 0.0))
                V(lambda e: e.memset(s_den[...], 1.0), inc=True)
                w.barrier()
                w.dma("gp", b_WkT[...], WkTd.rearrange("(fc p) n -> p fc n", p=128),
                      sem="W0")
                w.dma("gp", b_WvT[...], WvTd.rearrange("(fc p) n -> p fc n", p=128),
                      sem="W1")
                w.mark("kv_ld")
                for s in range(nsb):
                    c0 = 4 * s
                    w.wait("sy", "VE")
                    for j in range(4):
                        w.dma("sy", L1["x4"][:, j, :],
                              x_own[(c0 + j) * 128:(c0 + j + 1) * 128, :])
                    ln_stats(L1)
                    for j in range(4):
                        c = c0 + j
                        normalize_chunk(L1, j)
                        w.wait("ve", "ST")
                        V(lambda e, j=j: e.tensor_tensor(s_yst[:, j % 2, :],
                                                         L1["x4"][:, j, :],
                                                         L1["xn"][...], ALU.add),
                          inc=True)
                        w.wait("sy", "VE")
                        w.dma("sy", dr_yacc[c], s_yst[:, j % 2, :], store=True)
                        A(lambda e, j=j: e.activation(s_x2[...], L1["xn"][...],
                                                      AF.Identity,
                                                      scale=r_y2[:, j:j + 1]), inc=True)
                        w.wait("ve", "AC")
                        V(lambda e: e.tensor_tensor(s_x2s[...], s_x2[...], c_nsbc[...],
                                                    ALU.mult), inc=True)
                        w.wait("pe", "VE")
                        for fc in range(8):
                            P_(lambda e, fc=fc: e.transpose(
                                pXT[:, fc // 4, (fc % 4) * 128:(fc % 4 + 1) * 128],
                                s_x2s[:, fc * 128:(fc + 1) * 128], c_idef[...]),
                               inc=(fc == 7))
                        w.wait("ve", "PE")
                        V(lambda e: e.tensor_copy(
                            s_x2Tf[...], pXT[...].rearrange("p a (c b) -> p a c b",
                                                            b=128)), inc=True)
                        V(lambda e, j=j: e.tensor_copy(
                            s_x2Tb[:, :, j * 128:(j + 1) * 128]
                            .rearrange("p (a c) b -> p a c b", a=2), s_x2Tf[...]),
                          inc=True)
                        w.wait("pe", "VE")
                        for k in range(8):
                            P_(lambda e, k=k: e.matmul(pMM1[:, 0:32],
                                                       s_x2Tf[:, k // 4, k % 4, :],
                                                       c_wg[:, k, :], start=(k == 0),
                                                       stop=(k == 7)), inc=(k == 7))
                        w.wait("ve", "PE")
                        V(lambda e, j=j: e.tensor_tensor(s_log[:, j, :], pMM1[:, 0:32],
                                                         c_bgbc[...], ALU.add),
                          inc=True)
                        w.wait("pe", "VE")
                    w.wait("sy", "VE")
                    w.dma("sy", dr_xnT[s].rearrange("m p t -> p m t"), L1["xnT"][...],
                          store=True)
                    w.dma("sy", dr_x2T[s].rearrange("m p t -> p m t"), s_x2Tb[...],
                          store=True)
                    Vc = w.vc
                    for j in range(4):
                        V(lambda e, j=j: e.tensor_copy(s_sel[:, j, :], s_log[:, j, :]),
                          inc=True)
                    w.wait("ve", "VE")
                    for r in range(4):
                        for j in range(4):
                            V(lambda e, r=r, j=j: e.reduce_max(
                                s_mx[:, 4 * r + j:4 * r + j + 1], s_sel[:, j, :],
                                axis=AX.X), inc=True)
                        w.wait("ve", "VE")
                        if r < 3:
                            for j in range(4):
                                V(lambda e, r=r, j=j: e.tensor_scalar(
                                    s_es[:, j, :], s_sel[:, j, :],
                                    s_mx[:, 4 * r + j:4 * r + j + 1], None,
                                    ALU.is_equal), inc=True)
                            w.wait("ve", "VE")
                            for j in range(4):
                                V(lambda e, j=j: e.scalar_tensor_tensor(
                                    s_sel[:, j, :], s_es[:, j, :], NEG, s_sel[:, j, :],
                                    ALU.mult, ALU.add), inc=True)
                            w.wait("ve", "VE")
                    for j in range(4):
                        V(lambda e, j=j: e.tensor_scalar(s_sel[:, j, :], s_log[:, j, :],
                                                         s_mx[:, 12 + j:13 + j], None,
                                                         ALU.is_ge), inc=True)
                    Vc(lambda e: e.tensor_scalar(s_hb[...], s_mx[...], -0.5, None,
                                                 ALU.mult))
                    w.wait("ac", "VE")
                    for j in range(4):
                        A(lambda e, j=j: e.activation(s_th[:, j, :], s_log[:, j, :],
                                                      AF.Tanh, scale=0.5,
                                                      bias=s_hb[:, j:j + 1]),
                          inc=(j == 3))
                    w.wait("ve", "AC")
                    for j in range(4):
                        V(lambda e, j=j: e.tensor_scalar(s_ea[:, j, :], s_th[:, j, :],
                                                         1.0, None, ALU.add), inc=True)
                    for j in range(4):
                        V(lambda e, j=j: e.tensor_scalar(s_eb[:, j, :], s_th[:, j, :],
                                                         -1.0, 1.0, ALU.mult, ALU.add),
                          inc=True)
                    w.wait("ve", "VE")
                    for j in range(4):
                        V(lambda e, j=j: e.reciprocal(s_es[:, j, :], s_eb[:, j, :]),
                          inc=True)
                    w.wait("ve", "VE")
                    for j in range(4):
                        V(lambda e, j=j: e.tensor_tensor(s_ea[:, j, :], s_ea[:, j, :],
                                                         s_es[:, j, :], ALU.mult),
                          inc=True)
                    w.wait("ve", "VE")
                    for j in range(4):
                        V(lambda e, j=j: e.tensor_tensor(s_ea[:, j, :], s_ea[:, j, :],
                                                         s_sel[:, j, :], ALU.mult),
                          inc=True)
                    w.wait("ve", "VE")
                    for j in range(4):
                        V(lambda e, j=j: e.reduce_sum(s_den[:, j:j + 1], s_ea[:, j, :],
                                                      axis=AX.X), inc=True)
                    w.wait("ve", "VE")
                    Vc(lambda e: e.reciprocal(s_rden[...], s_den[...]))
                    for j in range(4):
                        w.wait("ac", "VE")
                        A(lambda e, j=j: e.activation(s_g16[...], s_ea[:, j, :],
                                                      AF.Identity,
                                                      scale=s_rden[:, j:j + 1]),
                          inc=True)
                        w.wait("pe", "AC")
                        P_(lambda e: e.transpose(pG1[0:16, 0:128], s_g16[:, 0:16],
                                                 c_idef[...]), inc=True)
                        w.wait("ve", "PE")
                        V(lambda e, j=j, c0=c0: e.tensor_copy(
                            s_gateT[:, (c0 + j) * 128:(c0 + j + 1) * 128],
                            pG1[0:16, 0:128]), inc=True)
                        w.wait("pe", "VE")
                    w.barrier()

            if "F1b" in phases:
                w.dma("gp", b_WqT[...], WqTd.rearrange("(fc p) n -> p fc n", p=128),
                      sem="U0")
                w.mark("wq_ld")
                w.dma("gp", b_WgT[...], WgTd.rearrange("(fc p) n -> p fc n", p=128),
                      sem="U1")
                w.mark("wg_ld")
                w.dma("sy", b_xnT[:, 0, :, :], dr_xnT[0].rearrange("m p t -> p m t"))
                w.dma("sy", b_cq[:, 0, :], cosq[:, 0:512])
                w.dma("sy", b_sq[:, 0, :], sinq[:, 0:512])
                w.dma("sy", b_ck[:, 0, :], cosf[:, 0:512])
                w.dma("sy", b_sk[:, 0, :], sinf[:, 0:512])
                w.mark("bld_0")
                for s in range(nsb):
                    c0 = 4 * s
                    sp = s % 2
                    xnT = b_xnT[:, sp, :, :]
                    if s + 1 < nsb:
                        t0 = (s + 1) * 512
                        if s >= 1:
                            w.wmark("sy", "PE", f"gB_{s - 1}_3")
                            w.wmark("sy", "VE", f"rope_{s - 1}_7")
                        w.dma("sy", b_xnT[:, 1 - sp, :, :],
                              dr_xnT[s + 1].rearrange("m p t -> p m t"))
                        w.dma("sy", b_cq[:, 1 - sp, :], cosq[:, t0:t0 + 512])
                        w.dma("sy", b_sq[:, 1 - sp, :], sinq[:, t0:t0 + 512])
                        w.dma("sy", b_ck[:, 1 - sp, :], cosf[:, t0:t0 + 512])
                        w.dma("sy", b_sk[:, 1 - sp, :], sinf[:, t0:t0 + 512])
                        w.mark(f"bld_{s + 1}")
                    for h in range(H):
                        for ki, (wT, dst) in enumerate([(b_WqT, b_qT),
                                                        (b_WkT, b_kT)]):
                            u = 2 * h + ki
                            ct = (b_cq if ki == 0 else b_ck)[:, sp, :]
                            st = (b_sq if ki == 0 else b_sk)[:, sp, :]
                            pe1 = pMM1 if u % 2 == 0 else pY1
                            pe2 = pMM2 if u % 2 == 0 else pY2
                            m1, m2 = 2 * h, 2 * h + 1
                            if u == 0:
                                w.wmark("pe", "LD", f"bld_{s}")
                                w.wmark("pe", "U0", "wq_ld")
                                w.wmark("pe", "W0", "kv_ld")
                                w.wmark("pe", "W1", "kv_ld")
                                if s == 0:
                                    w.wmark("pe", "AC", "gB_m1")
                                else:
                                    w.wmark("pe", "AC", f"gA_{s - 1}_3")
                            if u == 1 and s > 0:
                                w.wmark("pe", "AC", f"gB_{s - 1}_3e")
                            if u >= 2:
                                w.wmark("pe", "VE", f"rope_{s}_{u - 2}")
                            for k in range(8):
                                P_(lambda e, k=k, m1=m1, wT=wT, pe1=pe1, xnT=xnT:
                                   e.matmul(pe1[...],
                                            wT[:, k, m1 * 128:(m1 + 1) * 128],
                                            xnT[:, k, :], start=(k == 0),
                                            stop=(k == 7)))
                            for k in range(8):
                                P_(lambda e, k=k, m2=m2, wT=wT, pe2=pe2, xnT=xnT:
                                   e.matmul(pe2[...],
                                            wT[:, k, m2 * 128:(m2 + 1) * 128],
                                            xnT[:, k, :], start=(k == 0),
                                            stop=(k == 7)), inc=(k == 7))
                            w.mark(f"qk_{s}_{u}")
                            w.wmark("ve", "PE", f"qk_{s}_{u}")
                            if u <= 1:
                                w.wait("ve", "ST")
                            V(lambda e, ct=ct, pe1=pe1: e.tensor_tensor(
                                b_rt1[...], pe1[...], ct, ALU.mult))
                            V(lambda e, st=st, pe2=pe2: e.tensor_tensor(
                                b_rt2[...], pe2[...], st, ALU.mult))
                            V(lambda e, dst=dst, m1=m1: e.tensor_tensor(
                                dst[:, m1, :], b_rt1[...], b_rt2[...],
                                ALU.subtract))
                            V(lambda e, ct=ct, pe2=pe2: e.tensor_tensor(
                                b_rt1[...], pe2[...], ct, ALU.mult))
                            V(lambda e, st=st, pe1=pe1: e.tensor_tensor(
                                b_rt2[...], pe1[...], st, ALU.mult))
                            V(lambda e, dst=dst, m2=m2: e.tensor_tensor(
                                dst[:, m2, :], b_rt1[...], b_rt2[...], ALU.add),
                              inc=True)
                            w.mark(f"rope_{s}_{u}")
                    w.wmark("sy", "VE", f"rope_{s}_7")
                    w.dma("sy", dr_qT[s].rearrange("m p t -> p m t"), b_qT[...],
                          store=True)
                    w.dma("sy", dr_kT[s].rearrange("m p t -> p m t"), b_kT[...],
                          store=True)
                    for j in range(4):
                        c = c0 + j
                        if j == 0:
                            w.wmark("pe", "VE", f"rope_{s}_6")
                        else:
                            w.wmark("pe", "AC", f"gA_{s}_{j - 1}")
                        for bk in range(2):
                            pb = pMM1 if bk == 0 else pMM2
                            for k in range(8):
                                P_(lambda e, k=k, bk=bk, pb=pb, j=j, xnT=xnT:
                                   e.matmul(pb[...],
                                            xnT[:, k, j * 128:(j + 1) * 128],
                                            b_WvT[:, k, bk * 512:(bk + 1) * 512],
                                            start=(k == 0), stop=(k == 7)),
                                   inc=(k == 7 and bk == 1))
                        w.mark(f"vA_{s}_{j}")
                        if j == 0:
                            w.wmark("pe", "VE", f"rope_{s}_7")
                        else:
                            w.wmark("pe", "AC", f"gB_{s}_{j - 1}e")
                        for bk in range(2, 4):
                            pb = pY1 if bk == 2 else pY2
                            for k in range(8):
                                P_(lambda e, k=k, bk=bk, pb=pb, j=j, xnT=xnT:
                                   e.matmul(pb[...],
                                            xnT[:, k, j * 128:(j + 1) * 128],
                                            b_WvT[:, k, bk * 512:(bk + 1) * 512],
                                            start=(k == 0), stop=(k == 7)),
                                   inc=(k == 7 and bk == 3))
                        w.mark(f"vB_{s}_{j}")
                        w.wmark("ve", "PE", f"vA_{s}_{j}")
                        w.wait("ve", "ST")
                        for bk in range(2):
                            pb = pMM1 if bk == 0 else pMM2
                            V(lambda e, bk=bk, pb=pb: e.tensor_copy(
                                b_v[:, bk * 512:(bk + 1) * 512], pb[...]),
                              inc=(bk == 1))
                        w.mark(f"vAe_{s}_{j}")
                        w.wmark("ve", "PE", f"vB_{s}_{j}")
                        for bk in range(2, 4):
                            pb = pY1 if bk == 2 else pY2
                            V(lambda e, bk=bk, pb=pb: e.tensor_copy(
                                b_v[:, bk * 512:(bk + 1) * 512], pb[...]),
                              inc=(bk == 3))
                        w.mark(f"vBe_{s}_{j}")
                        w.wmark("sy", "VE", f"vBe_{s}_{j}")
                        w.dma("sy", dr_v[c], b_v[...], store=True)
                        w.wmark("pe", "VE", f"vAe_{s}_{j}")
                        if j == 0:
                            w.wmark("pe", "U1", "wg_ld")
                        for bk in range(2):
                            pb = pMM1 if bk == 0 else pMM2
                            for k in range(8):
                                P_(lambda e, k=k, bk=bk, pb=pb, j=j, xnT=xnT:
                                   e.matmul(pb[...],
                                            xnT[:, k, j * 128:(j + 1) * 128],
                                            b_WgT[:, k, bk * 512:(bk + 1) * 512],
                                            start=(k == 0), stop=(k == 7)),
                                   inc=(k == 7 and bk == 1))
                        w.mark(f"gA_{s}_{j}m")
                        w.wmark("pe", "VE", f"vBe_{s}_{j}")
                        for bk in range(2, 4):
                            pb = pY1 if bk == 2 else pY2
                            for k in range(8):
                                P_(lambda e, k=k, bk=bk, pb=pb, j=j, xnT=xnT:
                                   e.matmul(pb[...],
                                            xnT[:, k, j * 128:(j + 1) * 128],
                                            b_WgT[:, k, bk * 512:(bk + 1) * 512],
                                            start=(k == 0), stop=(k == 7)),
                                   inc=(k == 7 and bk == 3))
                        w.mark(f"gB_{s}_{j}")
                        w.wmark("ac", "PE", f"gA_{s}_{j}m")
                        w.wait("ac", "ST")
                        for bk in range(2):
                            pb = pMM1 if bk == 0 else pMM2
                            A(lambda e, bk=bk, pb=pb: e.activation(
                                b_sg[:, bk * 512:(bk + 1) * 512], pb[...],
                                AF.Silu), inc=(bk == 1))
                        w.mark(f"gA_{s}_{j}")
                        w.wmark("ac", "PE", f"gB_{s}_{j}")
                        for bk in range(2, 4):
                            pb = pY1 if bk == 2 else pY2
                            A(lambda e, bk=bk, pb=pb: e.activation(
                                b_sg[:, bk * 512:(bk + 1) * 512], pb[...],
                                AF.Silu), inc=(bk == 3))
                        w.mark(f"gB_{s}_{j}e")
                        w.wmark("sy", "AC", f"gB_{s}_{j}e")
                        w.dma("sy", dr_sig[c], b_sg[...], store=True)
                w.barrier()

            if "P" in phases:
                if "F1b" not in phases and "F1a" not in phases:
                    w.dma("gp", b_WkT[...],
                          WkTd.rearrange("(fc p) n -> p fc n", p=128), sem="W0")
                    w.dma("gp", b_WvT[...],
                          WvTd.rearrange("(fc p) n -> p fc n", p=128), sem="W1")
                w.dma("sy", p_kdec[...], kdeci.transpose([1, 0, 2]))
                V(lambda e: e.memset(s_S[...], 0.0))
                V(lambda e: e.memset(LP["S1"][...], 0.0))
                V(lambda e: e.memset(LP["S2"][...], 0.0), inc=True)
                w.barrier()
                for s in range(nsb):
                    c0 = 4 * s
                    w.wait("sy", "VE")
                    for j in range(4):
                        w.dma("sy", LP["x4"][:, j, :],
                              x_pre[(c0 + j) * 128:(c0 + j + 1) * 128, :])
                    ln_stats(LP, rms=False)
                    for j in range(4):
                        c = c0 + j
                        normalize_chunk(LP, j)
                        w.wait("sy", "VE")
                        w.dma("sy", p_ct[...], costm[c * 128:(c + 1) * 128, :])
                        w.dma("sy", p_st[...], sintm[c * 128:(c + 1) * 128, :])
                        w.wait("pe", "VE")
                        for nb, pb in enumerate([pMM1, pMM2]):
                            for k in range(8):
                                P_(lambda e, k=k, nb=nb, pb=pb, j=j: e.matmul(
                                    pb[...], LP["xnT"][:, k, j * 128:(j + 1) * 128],
                                    b_WkT[:, k, nb * 512:(nb + 1) * 512],
                                    start=(k == 0), stop=(k == 7)),
                                   inc=(k == 7 and nb == 1))
                        w.wait("ve", "PE")
                        w.wait("ve", "LD")
                        w.vc(lambda e: e.tensor_copy(p_k[:, 0:512], pMM1[...]))
                        w.vc(lambda e: e.tensor_copy(p_k[:, 512:1024], pMM2[...]))
                        w.wait("pe", "VE")
                        for h in range(H):
                            hb = h * 256
                            V(lambda e, hb=hb: e.tensor_tensor(
                                p_rt1[:, 0:128], p_k[:, hb:hb + 128], p_ct[...],
                                ALU.mult))
                            V(lambda e, hb=hb: e.tensor_tensor(
                                p_rt1[:, 128:256], p_k[:, hb + 128:hb + 256],
                                p_st[...], ALU.mult))
                            V(lambda e, hb=hb: e.tensor_tensor(
                                p_rt2[:, 0:128], p_k[:, hb + 128:hb + 256], p_ct[...],
                                ALU.mult))
                            V(lambda e, hb=hb: e.tensor_tensor(
                                p_rt2[:, 128:256], p_k[:, hb:hb + 128], p_st[...],
                                ALU.mult))
                            V(lambda e, hb=hb: e.tensor_tensor(
                                p_k[:, hb:hb + 128], p_rt1[:, 0:128],
                                p_rt1[:, 128:256], ALU.subtract))
                            V(lambda e, hb=hb: e.tensor_tensor(
                                p_k[:, hb + 128:hb + 256], p_rt2[:, 0:128],
                                p_rt2[:, 128:256], ALU.add), inc=(h == H - 1))
                        for h in range(H):
                            V(lambda e, h=h, j=j: e.tensor_scalar(
                                p_kp[:, j, h * 256:(h + 1) * 256],
                                p_k[:, h * 256:(h + 1) * 256],
                                p_kdec[:, j % 2, h:h + 1], None, ALU.mult),
                              inc=(h == H - 1))
                        for bk, pb in enumerate([pMM1, pMM2, pY1, pY2]):
                            for k in range(8):
                                P_(lambda e, k=k, bk=bk, pb=pb, j=j: e.matmul(
                                    pb[...], LP["xnT"][:, k, j * 128:(j + 1) * 128],
                                    b_WvT[:, k, bk * 512:(bk + 1) * 512],
                                    start=(k == 0), stop=(k == 7)),
                                   inc=(k == 7 and bk == 3))
                        w.wait("ve", "PE")
                        for bk, pb in enumerate([pMM1, pMM2, pY1, pY2]):
                            V(lambda e, bk=bk, pb=pb, j=j: e.tensor_copy(
                                p_v[:, j, bk * 512:(bk + 1) * 512], pb[...]),
                              inc=(bk == 3))
                        w.wait("pe", "VE")
                    w.wait("pe", "AC")
                    for cc in range(2):
                        for h in range(H):
                            for dc in range(2):
                                for jj in range(2):
                                    P_(lambda e, cc=cc, h=h, dc=dc, jj=jj: e.matmul(
                                        pXT[:, dc, :],
                                        p_kp[:, cc * 2 + jj,
                                             h * 256 + dc * 128:
                                             h * 256 + (dc + 1) * 128],
                                        p_v[:, cc * 2 + jj, h * 512:(h + 1) * 512],
                                        start=(jj == 0), stop=(jj == 1)),
                                       inc=(jj == 1 and dc == 1))
                            w.wait("ve", "PE")
                            for dc in range(2):
                                V(lambda e, h=h, dc=dc: e.scalar_tensor_tensor(
                                    s_S[:, h, dc, :], s_S[:, h, dc, :], GC[h],
                                    pXT[:, dc, :], ALU.mult, ALU.add), inc=(dc == 1))
                            w.wait("pe", "VE")
                    w.barrier()
                for h in range(H):
                    for dc in range(2):
                        V(lambda e, h=h, dc=dc: e.tensor_copy(s_Sb[:, h, dc, :],
                                                              s_S[:, h, dc, :]),
                          inc=(h == H - 1 and dc == 1))
                w.wait("sy", "VE")
                w.dma("sy", dr_S[...], s_S[...], store=True)
                w.dma("sy", dr_Sb[...], s_Sb[...], store=True)
                w.barrier()

            if "F2" in phases:
                NI = 8 * nblk
                NY = 8 * nblk
                NEH = nq * NE

                def w1src(e, ic, ab):
                    c0 = ab * 1024 + ic * 128
                    return w1p[e][:, c0:c0 + 128].rearrange(
                        "(fc p) x -> p fc x", p=128)

                def w2src(e, hc):
                    return w2d[e][:, hc * 128:(hc + 1) * 128].rearrange(
                        "(ic p) x -> p ic x", p=128)

                for hf in range(chq // 4):
                    w.dma("sy", f_x2q[:, :, hf * 512:(hf + 1) * 512],
                          dr_x2T[hf].rearrange("m p t -> p m t"), sem="X2")
                w.mark("x2_0")
                w.dma("gp", c_row[...], b1p[0].rearrange("(a b) -> a b", a=16),
                      sem="CR")
                w.mark("crow_0")
                for sl in range(2):
                    w.dma("gp", f_w1s[:, :, sl, 0, :], w1src(0, sl, 0),
                          sem=f"W{sl}")
                    w.dma("gp", f_w1s[:, :, sl, 1, :], w1src(0, sl, 1),
                          sem=f"W{sl}")
                    w.mark(f"w1_0_{sl}")
                    w.dma("gp", f_w2s[:, :, sl, :], w2src(0, sl), sem=f"U{sl}")
                    w.mark(f"w2_0_{sl}")
                w.dma("gp", f_w1s[:, :, 2, 0, :], w1src(0, 2, 0), sem="W2")
                w.dma("gp", f_w1s[:, :, 2, 1, :], w1src(0, 2, 1), sem="W2")
                w.mark("w1_0_2")
                w.barrier()

                for uid in range(NEH):
                    q, e_i = divmod(uid, NE)
                    cq0 = q * chq
                    last_e = (e_i == NE - 1)
                    w.wmark("pe", "CR", f"crow_{uid}")
                    w.wmark("pe", "VE", f"b1rdy_{uid - 1}")
                    if e_i == 0 and q > 0:
                        w.wmark("pe", "VE", f"hevdone_{q - 1}")
                    P_(lambda e: e.transpose(pXT[:, 0, 0:16], c_row[...],
                                             c_idef[0:16, 0:16]), inc=True)
                    w.mark(f"b1T_{uid}")
                    if uid < NEH - 1:
                        w.wmark("gp", "PE", f"b1T_{uid}")
                        w.dma("gp", c_row[...],
                              b1p[(uid + 1) % NE].rearrange("(a b) -> a b", a=16),
                              sem="CR")
                        w.mark(f"crow_{uid + 1}")
                    w.wmark("ve", "PE", f"b1T_{uid}")
                    w.wmark("ve", "AC", f"evac_{uid - 1}_{NI - 1}")
                    V(lambda e: e.tensor_copy(f_b1T[:, 0:16], pXT[:, 0, 0:16]))
                    V(lambda e: e.tensor_scalar(f_b1T[:, 16:32], pXT[:, 0, 0:16],
                                                0.5, 0.5, ALU.mult, ALU.add),
                      inc=True)
                    w.mark(f"b1rdy_{uid}")
                    for b in range(nblk):
                        gb = pG1[...] if b % 2 == 0 else pXT[:, 0, :]
                        if b < 2:
                            w.wmark("pe", "VE", f"gbc_{uid - 1}_{nblk - 2 + b}")
                        else:
                            w.wmark("pe", "VE", f"gbc_{uid}_{b - 2}")
                        if b % 2 == 1:
                            w.wmark("pe", "VE", f"b1rdy_{uid}")
                        P_(lambda e, e_i=e_i, gb=gb, t0=b * 512, cq0=cq0: e.matmul(
                            gb, c_E16[:, e_i * 128:(e_i + 1) * 128],
                            s_gateT[:, cq0 * 128 + t0:cq0 * 128 + t0 + 512],
                            start=True, stop=True), inc=True)
                        w.mark(f"gbcm_{uid}_{b}")
                        w.wmark("ve", "PE", f"gbcm_{uid}_{b}")
                        w.wmark("ve", "GP", f"s_{uid - 1}_{7 * nblk + b}")
                        V(lambda e, b=b, gb=gb: e.tensor_copy(f_gbc[:, b, :], gb),
                          inc=True)
                        w.mark(f"gbc_{uid}_{b}")
                    for ic in range(8):
                        for b in range(nblk):
                            gi = ic * nblk + b
                            pp = gi % 2
                            pa = pMM1 if pp == 0 else pMM2
                            pbb = pY1 if pp == 0 else pY2
                            if b == 0:
                                w.wmark("pe", f"W{(uid * 8 + ic) % 3}",
                                        f"w1_{uid}_{ic}")
                                if e_i == 0 and ic == 0:
                                    w.wmark("pe", "X2", f"x2_{q}")
                            if gi >= 2:
                                w.wmark("pe", "AC", f"evac_{uid}_{gi - 2}")
                            else:
                                w.wmark("pe", "AC", f"evac_{uid - 1}_{NI - 2 + gi}")
                                w.wmark("pe", "VE", f"yTe_{uid - 1}_{NY - 2 + gi}")
                            for k in range(8):
                                P_(lambda e, k=k, ic=ic, pa=pa, t0=b * 512:
                                   e.matmul(pa[...],
                                            f_w1s[:, k, (uid * 8 + ic) % 3, 0, :],
                                            f_x2q[:, k, t0:t0 + 512],
                                            start=(k == 0), stop=(k == 7)),
                                   inc=(k == 7))
                            for k in range(8):
                                P_(lambda e, k=k, ic=ic, pbb=pbb, t0=b * 512:
                                   e.matmul(pbb[...],
                                            f_w1s[:, k, (uid * 8 + ic) % 3, 1, :],
                                            f_x2q[:, k, t0:t0 + 512],
                                            start=(k == 0), stop=(k == 7)),
                                   inc=(k == 7))
                            w.mark(f"mm_{uid}_{gi}")
                            w.wmark("ac", "PE", f"mm_{uid}_{gi}")
                            if gi >= 2:
                                w.wmark("ac", "GP", f"s_{uid}_{gi - 2}")
                            else:
                                w.wmark("ac", "GP", f"s_{uid - 1}_{NI - 2 + gi}")
                            A(lambda e, ic=ic, pa=pa, pp=pp: e.activation(
                                f_aS[:, pp, :], pa[...], AF.Identity,
                                bias=f_b1T[:, ic:ic + 1]), inc=True)
                            A(lambda e, ic=ic, pbb=pbb, pp=pp: e.activation(
                                f_bS[:, pp, :], pbb[...], AF.Identity, scale=0.5,
                                bias=f_b1T[:, 24 + ic:25 + ic]), inc=True)
                            w.mark(f"evac_{uid}_{gi}")
                            w.wmark("ve", "AC", f"evac_{uid}_{gi}")
                            if gi >= 2:
                                w.wmark("ve", "GP", f"s_{uid}_{gi - 2}")
                            else:
                                w.wmark("ve", "GP", f"s_{uid - 1}_{NI - 2 + gi}")
                            V(lambda e, pp=pp: e.tensor_scalar(
                                f_ac[:, pp, :], f_aS[:, pp, :], LIMIT, None,
                                ALU.min), inc=True)
                            w.mark(f"acmin_{uid}_{gi}")
                            w.wmark("ac", "VE", f"acmin_{uid}_{gi}")
                            A(lambda e, pp=pp: e.activation(
                                f_aS[:, pp, :], f_ac[:, pp, :], AF.Tanh,
                                scale=ALPHA / 2.0), inc=True)
                            w.mark(f"tanh_{uid}_{gi}")
                            V(lambda e, pp=pp: e.tensor_scalar(
                                f_bS[:, pp, :], f_bS[:, pp, :], -3.0, 4.0,
                                ALU.max, ALU.min))
                            w.wmark("ve", "AC", f"tanh_{uid}_{gi}")
                            V(lambda e, pp=pp: e.scalar_tensor_tensor(
                                f_p1[:, pp, :], f_aS[:, pp, :], 1.0,
                                f_ac[:, pp, :], ALU.add, ALU.mult), inc=True)
                            w.mark(f"p1_{uid}_{gi}")
                            if gi == 0:
                                w.wmark("gp", "PE", f"yTm_{uid - 1}_{NY - 1}")
                            w.wmark("gp", "VE", f"p1_{uid}_{gi}")
                            w.wmark("gp", "VE", f"gbc_{uid}_{b}")
                            G(lambda e, pp=pp: e.tensor_tensor(
                                f_ac[:, pp, :], f_p1[:, pp, :], f_bS[:, pp, :],
                                ALU.mult))
                            G(lambda e, ic=ic, pp=pp, b=b: e.tensor_tensor(
                                f_s[:, b, ic, :], f_ac[:, pp, :], f_gbc[:, b, :],
                                ALU.mult), inc=True)
                            w.mark(f"s_{uid}_{gi}")
                        g = uid * 8 + ic
                        if g + 3 < NEH * 8:
                            tg = g + 3
                            ne, nic = divmod(tg, 8)
                            nsl = tg % 3
                            w.wmark("gp", "PE", f"mm_{uid}_{ic * nblk + nblk - 1}")
                            w.dma("gp", f_w1s[:, :, nsl, 0, :],
                                  w1src(ne % NE, nic, 0), sem=f"W{nsl}")
                            w.dma("gp", f_w1s[:, :, nsl, 1, :],
                                  w1src(ne % NE, nic, 1), sem=f"W{nsl}")
                            w.mark(f"w1_{ne}_{nic}")
                        if e_i == NE - 1 and ic == 7 and q + 1 < nq:
                            w.wmark("sy", "PE", f"mm_{uid}_{NI - 1}")
                            for hf in range(chq // 4):
                                w.dma("sy", f_x2q[:, :, hf * 512:(hf + 1) * 512],
                                      dr_x2T[(q + 1) * (chq // 4) + hf]
                                      .rearrange("m p t -> p m t"), sem="X2")
                            w.mark(f"x2_{q + 1}")
                    for hc in range(8):
                        for b in range(nblk):
                            yi = hc * nblk + b
                            pb = pY1 if yi % 2 == 0 else pY2
                            if b == 0:
                                w.wmark("pe", f"U{hc % 2}", f"w2_{uid}_{hc}")
                            if hc == 0:
                                w.wmark("pe", "GP", f"s_{uid}_{7 * nblk + b}")
                            if yi >= 2:
                                w.wmark("pe", "VE", f"yTe_{uid}_{yi - 2}")
                            else:
                                w.wmark("pe", "AC", f"evac_{uid}_{NI - 2 + yi}")
                                w.wmark("pe", "VE", f"yTe_{uid - 1}_{NY - 2 + yi}")
                            for k in range(8):
                                P_(lambda e, k=k, hc=hc, pb=pb, b=b: e.matmul(
                                    pb[...], f_w2s[:, k, hc % 2, :],
                                    f_s[:, b, k, :], start=(k == 0),
                                    stop=(k == 7 and not last_e)),
                                   inc=(k == 7 and not last_e))
                            if last_e:
                                P_(lambda e, hc=hc, pb=pb, t0=b * 512, cq0=cq0:
                                   e.matmul(
                                       pb[...],
                                       c_b2[:, hc * 128:(hc + 1) * 128],
                                       s_gateT[:, cq0 * 128 + t0:
                                               cq0 * 128 + t0 + 512],
                                       start=False, stop=True), inc=True)
                            w.mark(f"yTm_{uid}_{yi}")
                            w.wmark("ve", "PE", f"yTm_{uid}_{yi}")
                            if e_i == 0 and q > 0 and yi == 0:
                                w.wmark("ve", "PE", f"hevT_{q - 1}")
                            V(lambda e, hc=hc, pb=pb, e_i=e_i, b=b: (
                                e.tensor_copy(f_yT[:, b, hc, :], pb[...])
                                if e_i == 0 else
                                e.tensor_tensor(f_yT[:, b, hc, :],
                                                f_yT[:, b, hc, :], pb[...],
                                                ALU.add)), inc=True)
                            w.mark(f"yTe_{uid}_{yi}")
                        nhc = hc + 2
                        ne = uid + (1 if nhc >= 8 else 0)
                        if ne < NEH:
                            nhc = nhc % 8
                            w.wmark("gp", "PE", f"yTm_{uid}_{hc * nblk + nblk - 1}")
                            w.dma("gp", f_w2s[:, :, nhc % 2, :],
                                  w2src(ne % NE, nhc), sem=f"U{nhc % 2}")
                            w.mark(f"w2_{ne}_{nhc}")
                    if last_e:
                        w.wmark("pe", "VE", f"yTe_{uid}_{NY - 1}")
                        for cxl in range(chq):
                            cx = cq0 + cxl
                            b, tc = divmod(cxl, 4)
                            w.wait("sy", "VE")
                            w.dma("sy", f_yst2[:, cxl % 2, :], dr_yacc[cx])
                            for hcc in range(8):
                                pt = pG1 if hcc % 2 == 0 else pXT
                                ptap = (pt[:, 0:128] if hcc % 2 == 0
                                        else pt[:, 0, 0:128])
                                P_(lambda e, b=b, hcc=hcc, tc=tc, ptap=ptap:
                                   e.transpose(
                                       ptap,
                                       f_yT[:, b, hcc, tc * 128:(tc + 1) * 128],
                                       c_idef[...]), inc=True)
                                w.wait("ve", "PE")
                                w.wait("ve", "LD")
                                V(lambda e, hcc=hcc, cxl=cxl, ptap=ptap:
                                  e.tensor_tensor(
                                      f_yst2[:, cxl % 2,
                                             hcc * 128:(hcc + 1) * 128],
                                      f_yst2[:, cxl % 2,
                                             hcc * 128:(hcc + 1) * 128],
                                      ptap, ALU.add), inc=True)
                                w.wait("pe", "VE")
                            w.wait("sy", "VE")
                            w.dma("sy", dr_yacc[cx], f_yst2[:, cxl % 2, :],
                                  store=True)
                        w.mark(f"hevT_{q}")
                        w.mark(f"hevdone_{q}")
                w.barrier()

            if dbg_yacc:
                for c in range(nch):
                    w.dma("sy", s_yst[:, c % 2, :], dr_yacc[c])
                    w.wait("sy", "LD")
                    w.dma("sy", out[c * 128:(c + 1) * 128, :], s_yst[:, c % 2, :],
                          store=True)
                if dbg_dump:
                    w.dma("sy", d_x2[...], dr_x2T[...], store=True)
                    w.dma("sy", d_yT[...], f_yT[...], store=True)
                    w.dma("sy", d_fs[...], f_s[...], store=True)
                    w.dma("sy", d_xq[...], f_x2q[...], store=True)
                w.wait("gp", "ST")
                w.wait("sy", "ST")
                return

            if "F3" in phases:
                w.dma("sy", g_mask[...], maskTi.transpose([1, 0, 2, 3]))
                w.dma("sy", g_kdec[...], kdeci.transpose([1, 0, 2]))
                w.dma("sy", g_qgam[...], qgami[...])
                w.dma("sy", g_S[...], dr_S[...])
                w.dma("sy", g_Sb[...], dr_Sb[...])
                V(lambda e: e.memset(g_ssq[...], 0.0), inc=True)
                for cc2 in range(16):
                    w.wait("sy", "AC")
                    w.dma("sy", g_wst[...], WoTd[cc2 * 128:(cc2 + 1) * 128, :])
                    w.wait("ac", "LD")
                    A(lambda e, cc2=cc2: e.activation(g_WoT[:, cc2, :], g_wst[...],
                                                      AF.Identity,
                                                      scale=c_gnwT[:, cc2:cc2 + 1]),
                      inc=True)
                w.barrier()
                for s in range(nsb):
                    c0 = 4 * s
                    w.wait("sy", "VE")
                    w.wait("sy", "PE")
                    w.dma("sy", g_qT[...], dr_qT[s].rearrange("m p t -> p m t"))
                    w.dma("sy", g_kT[...], dr_kT[s].rearrange("m p t -> p m t"))
                    w.barrier()
                    for cc in range(2):
                        w.wait("sy", "VE")
                        for jj in range(2):
                            w.dma("sy", g_v[:, jj, :], dr_v[c0 + cc * 2 + jj])
                            w.dma("sy", g_sig[:, jj, :], dr_sig[c0 + cc * 2 + jj])
                        for jj in range(2):
                            w.wait("pe", "AC")
                            for h in range(H):
                                for dc in range(2):
                                    sl = h * 2 + dc
                                    P_(lambda e, h=h, dc=dc, jj=jj, cc=cc, sl=sl:
                                       e.transpose(pTb[:, sl, :],
                                                   g_kT[:, 2 * h + dc,
                                                        cc * 256 + jj * 128:
                                                        cc * 256 + (jj + 1) * 128],
                                                   c_ideb[...]),
                                       inc=(h == H - 1 and dc == 1))
                            w.wait("ac", "PE")
                            for h in range(H):
                                for dc in range(2):
                                    sl = h * 2 + dc
                                    A(lambda e, h=h, dc=dc, jj=jj, sl=sl: e.activation(
                                        g_ktok[:, jj, sl, :], pTb[:, sl, :],
                                        AF.Identity, scale=g_kdec[:, jj, h:h + 1]),
                                      inc=(h == H - 1 and dc == 1))
                        for h in range(H):
                            ob = [pMM1, pMM2, pY1, pY2][h]
                            w.wait("ve", "LD")
                            w.wait("ve", "AC")
                            w.wait("ve", "PE")
                            V(lambda e, h=h, cc=cc: e.tensor_tensor(
                                g_qg[:, 0, :], g_qT[:, 2 * h, cc * 256:(cc + 1) * 256],
                                g_qgam[:, h, :], ALU.mult))
                            V(lambda e, h=h, cc=cc: e.tensor_tensor(
                                g_qg[:, 1, :],
                                g_qT[:, 2 * h + 1, cc * 256:(cc + 1) * 256],
                                g_qgam[:, h, :], ALU.mult), inc=True)
                            w.wait("pe", "VE")
                            for jc in range(2):
                                for dc in range(2):
                                    P_(lambda e, h=h, cc=cc, jc=jc, dc=dc: e.matmul(
                                        pG1[:, jc * 256:(jc + 1) * 256],
                                        g_kT[:, 2 * h + dc,
                                             cc * 256 + jc * 128:
                                             cc * 256 + (jc + 1) * 128],
                                        g_qT[:, 2 * h + dc, cc * 256:(cc + 1) * 256],
                                        start=(dc == 0), stop=(dc == 1)),
                                       inc=(jc == 1 and dc == 1))
                            w.wait("ve", "PE")
                            for jc in range(2):
                                V(lambda e, h=h, jc=jc: e.tensor_tensor(
                                    g_am[:, h, jc, :], pG1[:, jc * 256:(jc + 1) * 256],
                                    g_mask[:, jc, h, :], ALU.mult), inc=(jc == 1))
                            w.wait("pe", "VE")
                            for qq in range(2):
                                for jc in range(2):
                                    P_(lambda e, h=h, qq=qq, jc=jc, ob=ob: e.matmul(
                                        ob[...],
                                        g_am[:, h, jc, qq * 128:(qq + 1) * 128],
                                        g_v[:, jc, h * 512:(h + 1) * 512],
                                        start=(jc == 0), stop=False))
                                for dc in range(2):
                                    P_(lambda e, h=h, qq=qq, dc=dc, ob=ob: e.matmul(
                                        ob[...], g_qg[:, dc, qq * 128:(qq + 1) * 128],
                                        g_Sb[:, h, dc, :], start=False,
                                        stop=(dc == 1)), inc=(dc == 1))
                                w.wait("ac", "PE")
                                w.wait("ac", "VE")
                                w.wait("ac", "AC")
                                A(lambda e, h=h, qq=qq, ob=ob: e.activation(
                                    g_junk[:, h, :], ob[...], AF.Square,
                                    accum_out=g_ssq[:, 4 * qq + h:4 * qq + h + 1]),
                                  inc=True)
                                A(lambda e, h=h, qq=qq, ob=ob: e.activation(
                                    g_osc[:, qq, h * 512:(h + 1) * 512], ob[...],
                                    AF.Identity), inc=True)
                                w.wait("pe", "AC")
                        for h in range(H):
                            for dc in range(2):
                                for jj in range(2):
                                    P_(lambda e, h=h, dc=dc, jj=jj: e.matmul(
                                        pXT[:, dc, :], g_ktok[:, jj, h * 2 + dc, :],
                                        g_v[:, jj, h * 512:(h + 1) * 512],
                                        start=(jj == 0), stop=(jj == 1)),
                                       inc=(jj == 1 and dc == 1))
                            w.wait("ve", "PE")
                            for dc in range(2):
                                V(lambda e, h=h, dc=dc: e.scalar_tensor_tensor(
                                    g_S[:, h, dc, :], g_S[:, h, dc, :], GC[h],
                                    pXT[:, dc, :], ALU.mult, ALU.add))
                            for dc in range(2):
                                V(lambda e, h=h, dc=dc: e.tensor_copy(
                                    g_Sb[:, h, dc, :], g_S[:, h, dc, :]),
                                  inc=(dc == 1))
                            w.wait("pe", "VE")
                        w.wait("ve", "AC")
                        if dbg_dump:
                            w.wait("sy", "AC")
                            ccx = s * 2 + cc
                            w.dma("sy", d_osc[c0 + cc * 2], g_osc[:, 0, :], store=True)
                            w.dma("sy", d_osc[c0 + cc * 2 + 1], g_osc[:, 1, :], store=True)
                            w.dma("sy", d_ssq[ccx], g_ssq[...], store=True)
                        w.vc(lambda e: e.tensor_scalar(r_a[...], g_ssq[...], 1.0 / DV,
                                                       EPS, ALU.mult, ALU.add))
                        rsqrt_chain()
                        V(lambda e: e.tensor_copy(r_t[...], r_y[...]), inc=True)
                        if dbg_dump:
                            w.wait("sy", "VE")
                            w.dma("sy", d_ry[s * 2 + cc], r_y[...], store=True)
                        for qq in range(2):
                            c = c0 + cc * 2 + qq
                            w.dma("sy", g_yre[:, qq, :], dr_yacc[c])
                            for h in range(H):
                                w.wait("ac", "VE")
                                A(lambda e, qq=qq, h=h: e.activation(
                                    g_wst[:, 0:512],
                                    g_osc[:, qq, h * 512:(h + 1) * 512], AF.Identity,
                                    scale=r_y[:, 4 * qq + h:4 * qq + h + 1]),
                                  inc=True)
                                w.wait("ve", "AC")
                                V(lambda e, qq=qq, h=h: e.tensor_tensor(
                                    g_onb[:, h * 512:(h + 1) * 512], g_wst[:, 0:512],
                                    g_sig[:, qq, h * 512:(h + 1) * 512], ALU.mult),
                                  inc=True)
                            w.wait("pe", "VE")
                            for half8 in range(2):
                                for t8 in range(8):
                                    cc2 = half8 * 8 + t8
                                    P_(lambda e, cc2=cc2, t8=t8: e.transpose(
                                        pTb[:, t8, :],
                                        g_onb[:, cc2 * 128:(cc2 + 1) * 128],
                                        c_ideb[...]), inc=(t8 == 7))
                                w.wait("ve", "PE")
                                V(lambda e, half8=half8: e.tensor_copy(
                                    g_onT[:, half8 * 8:(half8 + 1) * 8, :], pTb[...]),
                                  inc=True)
                                w.wait("pe", "VE")
                            for nb in range(2):
                                pb = pG1 if nb == 0 else pXT
                                pbap = pb[...] if nb == 0 else pb[:, 0, :]
                                for cc2 in range(16):
                                    P_(lambda e, nb=nb, cc2=cc2, pbap=pbap: e.matmul(
                                        pbap, g_onT[:, cc2, :],
                                        g_WoT[:, cc2, nb * 512:(nb + 1) * 512],
                                        start=(cc2 == 0), stop=(cc2 == 15)),
                                       inc=(cc2 == 15))
                            w.wait("ve", "PE")
                            w.wait("ve", "LD")
                            w.wait("ve", "ST")
                            V(lambda e, qq=qq: e.tensor_tensor(
                                g_fin[:, qq, 0:512], g_yre[:, qq, 0:512], pG1[...],
                                ALU.add))
                            V(lambda e, qq=qq: e.tensor_tensor(
                                g_fin[:, qq, 512:1024], g_yre[:, qq, 512:1024],
                                pXT[:, 0, :], ALU.add), inc=True)
                            w.wait("sy", "VE")
                            w.dma("sy", out[c * 128:(c + 1) * 128, :], g_fin[:, qq, :],
                                  store=True)
                            w.wait("pe", "VE")
                    w.barrier()

            if dbg_dump:
                w.dma("sy", d_v[...], dr_v[...], store=True)
                w.dma("sy", d_sig[...], dr_sig[...], store=True)
                w.dma("sy", d_qT[...], dr_qT[...], store=True)
                w.dma("sy", d_kT[...], dr_kT[...], store=True)
                w.dma("sy", d_S[...], g_S[...], store=True)
                w.dma("sy", d_kp[...], p_kp[...], store=True)
                w.dma("sy", d_pk[...], p_k[...], store=True)
                w.dma("sy", d_pv[...], p_v[...], store=True)
            w.wait("gp", "ST")
            w.wait("sy", "ST")

        @blk.sync
        def _(eng):
            walk(Walk("sy", eng))

        @blk.gpsimd
        def _(eng):
            walk(Walk("gp", eng))

        @blk.tensor
        def _(eng):
            walk(Walk("pe", eng))

        @blk.vector
        def _(eng):
            walk(Walk("ve", eng))

        @blk.scalar
        def _(eng):
            walk(Walk("ac", eng))

    nc.finalize()
    return nc



def _host_tables(T, pos0):
    pos = np.arange(pos0, pos0 + T, dtype=np.float64)
    invf = 1.0 / (10000.0 ** (np.arange(0, DK, 2, dtype=np.float64) / DK))
    fr = pos[:, None] * invf[None, :]
    cos = np.cos(fr)
    sin = np.sin(fr)
    sc = 1.0 / np.sqrt(DK)
    return (np.ascontiguousarray((cos.T * sc).astype(bfl)),
            np.ascontiguousarray((sin.T * sc).astype(bfl)),
            np.ascontiguousarray(cos.T.astype(bfl)),
            np.ascontiguousarray(sin.T.astype(bfl)),
            np.ascontiguousarray(cos.astype(bfl)),
            np.ascontiguousarray(sin.astype(bfl)))


def _host_consts():
    consts = {}
    consts["ones1"] = np.ones((1, 128), np.float32)
    consts["ide_b"] = np.eye(128, dtype=bfl)
    consts["ide_f"] = np.eye(128, dtype=np.float32)
    e16 = np.zeros((16, 2048), np.float32)
    for e in range(16):
        e16[e, e * 128:(e + 1) * 128] = 1.0
    consts["E16t"] = e16
    j = np.arange(CP)
    i = np.arange(CP)
    diff = i[None, :] - j[:, None]
    mask = np.zeros((2, 128, H, CP), np.float64)
    for h in range(H):
        m = np.where(diff >= 0, np.float64(GAMMA[h]) ** np.maximum(diff, 0), 0.0)
        mask[0, :, h, :] = m[0:128]
        mask[1, :, h, :] = m[128:256]
    consts["maskTi"] = mask.astype(bfl)
    kd = np.zeros((2, 128, H), np.float64)
    for h in range(H):
        v = np.float64(GAMMA[h]) ** (CP - 1.0 - j)
        kd[0, :, h] = v[0:128]
        kd[1, :, h] = v[128:256]
    consts["kdeci"] = kd.astype(np.float32)
    qg = np.zeros((128, H, CP), np.float64)
    for h in range(H):
        qg[:, h, :] = (np.float64(GAMMA[h]) ** (i + 1.0))[None, :]
    consts["qgami"] = qg.astype(bfl)
    return consts


def kernel(L0_attn_x, L0_attn_Wq, L0_attn_Wk, L0_attn_Wv, L0_attn_Wg, L0_attn_Wo,
           L0_attn_gnorm_w, L0_ffn_norm_scale, L0_ffn_wg, L0_ffn_bg, L0_ffn_w1,
           L0_ffn_b1, L0_ffn_w2, L0_ffn_b2):
    x = np.asarray(L0_attn_x, np.float32)
    B, TT, _ = x.shape
    Th = TT // 2
    nch = Th // 128

    perm = np.concatenate([np.arange(0, 2048, 2), np.arange(1, 2048, 2)])
    common = {
        "WqT": np.ascontiguousarray(np.asarray(L0_attn_Wq, np.float32).T),
        "WkT": np.ascontiguousarray(np.asarray(L0_attn_Wk, np.float32).T),
        "WvT": np.ascontiguousarray(np.asarray(L0_attn_Wv, np.float32).T),
        "WgT": np.ascontiguousarray(np.asarray(L0_attn_Wg, np.float32).T),
        "WoTd": np.ascontiguousarray(np.asarray(L0_attn_Wo, np.float32).T),
        "w1p": np.ascontiguousarray(np.asarray(L0_ffn_w1, np.float32)[:, :, perm]),
        "w2d": np.ascontiguousarray(np.asarray(L0_ffn_w2, np.float32)),
        "b1p": np.ascontiguousarray(np.asarray(L0_ffn_b1, np.float32)[:, perm]),
        "b2d": np.ascontiguousarray(np.asarray(L0_ffn_b2, np.float32)),
        "wgp": np.ascontiguousarray(
            np.concatenate([np.asarray(L0_ffn_wg, np.float32),
                            np.zeros((D, 16), np.float32)], axis=1)),
        "bg32": np.ascontiguousarray(
            np.concatenate([np.asarray(L0_ffn_bg, np.float32).reshape(1, 16),
                            np.full((1, 16), NEG, np.float32)], axis=1)),
        "gnw16": np.ascontiguousarray(
            np.tile(np.asarray(L0_attn_gnorm_w, np.float32), H).reshape(16, 128)),
        "ns_r": np.ascontiguousarray(
            np.asarray(L0_ffn_norm_scale, np.float32).reshape(1, D)),
    }
    common.update(_host_consts())
    cq0, sq0, cf0, sf0, ctm, stm = _host_tables(Th, 0)
    cq1, sq1, cf1, sf1, _, _ = _host_tables(Th, Th)

    nc = build_program(nch=nch)
    in_maps = []
    zeros_pre = np.zeros((Th, D), np.float32)
    for core in range(8):
        b, half = core // 2, core % 2
        m = dict(common)
        m["x_own"] = np.ascontiguousarray(x[b, half * Th:(half + 1) * Th])
        m["x_pre"] = np.ascontiguousarray(x[b, 0:Th]) if half else zeros_pre
        if half:
            m["cosq"], m["sinq"], m["cosf"], m["sinf"] = cq1, sq1, cf1, sf1
        else:
            m["cosq"], m["sinq"], m["cosf"], m["sinf"] = cq0, sq0, cf0, sf0
        m["costm"], m["sintm"] = ctm, stm
        in_maps.append(m)

    res = run_bass_kernel_spmd(nc, in_maps, core_ids=list(range(8)))
    outp = np.empty((B, TT, D), np.float32)
    for core in range(8):
        b, half = core // 2, core % 2
        outp[b, half * Th:(half + 1) * Th] = res.results[core]["out"]
    return outp
